# version 26
# baseline (speedup 1.0000x reference)
import contextlib
import numpy as np
import concourse.bass as bass
import concourse.mybir as mybir
from concourse.bass_utils import run_bass_kernel_spmd

F32 = mybir.dt.float32
BF16 = mybir.dt.bfloat16
I32 = mybir.dt.int32
AF = mybir.ActivationFunctionType
ALU = mybir.AluOpType
AX = mybir.AxisListType
AP = bass.AP

CFG = dict(L=4096, M=1024, S=64, I=2816, TC=512, TC3=256)
N_CORES = 8


class _FakeInst:
    def then_inc(self, sem, n=1):
        return self


_FAKE_INST = _FakeInst()


class _FakeEngine:
    def __getattr__(self, name):
        return lambda *a, **k: _FAKE_INST


class Sem:

    def __init__(self, handle):
        self.h = handle
        self.n = 0

    def inc(self, inst, k=1):
        inst.then_inc(self.h, k)
        self.n += k

    def wait(self, eng, val=None):
        eng.wait_ge(self.h, self.n if val is None else val)


class Res:

    def __init__(self, nslots, dma=False):
        self.k = nslots
        self.dma = dma
        self.p = [None] * nslots

    def acq(self, eng, idx):
        g = self.p[idx % self.k]
        if g is not None:
            g[0].wait(eng, None if self.dma else g[1])

    def rel(self, idx, sem):
        self.p[idx % self.k] = (sem, sem.n)


class View:

    def __init__(self, t, off, cols):
        self.t, self.off, self.cols = t, off, cols

    def __getitem__(self, key):
        p, c = key
        start = self.off + (c.start or 0)
        stop = self.off + (c.stop if c.stop is not None else self.cols)
        if c.step in (None, 1):
            return self.t[p, start:stop]
        return self.t[p, start:stop:c.step]


def build_nc(cfg):
    L, M, S, I = cfg["L"], cfg["M"], cfg["S"], cfg["I"]
    TC, TC3 = cfg["TC"], cfg["TC3"]
    B = N_CORES
    MP, NP, IP = M // 128, 2 * M // 128, I // 128
    LT = L // 128
    NCH = L // TC
    NC3 = L // TC3
    CT = TC // 128
    CT3 = TC3 // 128
    W = L // B
    GSZ = min(M, 512)
    GN = -(-M // GSZ)
    TL = 128 * 128
    QP = 512
    AFC = max(MP * S + M, 2048 + 4 * M + 2 * TC, 4 * TC + I, L,
              3 * 1024, 4 * TC, 4 * M + 9 * TC3)
    ABC = max(2 * I + 5 * MP * 128, 4 * MP * 128 + 2 * I,
              max(2 * I + 2 * MP * 128, 4 * MP * 128 + I) + I,
              M + 2 * MP * TC, MP * TC + 2 * MP * 128 + L, 2 * IP * TC3)

    nc = bass.Bass("TRN2", target_bir_lowering=False)

    xb = nc.dram_tensor("xb", [L, M], F32, kind="ExternalInput")
    xt = nc.dram_tensor("xt", [L, M], F32, kind="ExternalInput")
    winT = nc.dram_tensor("winT", [M, 2 * M], F32, kind="ExternalInput")
    wxpT = nc.dram_tensor("wxpT", [M, S], F32, kind="ExternalInput")
    woutT = nc.dram_tensor("woutT", [M, M], F32, kind="ExternalInput")
    dmat = nc.dram_tensor("dmat", [S, M], F32, kind="ExternalInput")
    TPC = -(-IP // B)
    IPAD = TPC * B
    wg_my = nc.dram_tensor("wg_my", [TPC * 128, M], F32, kind="ExternalInput")
    wu_my = nc.dram_tensor("wu_my", [TPC * 128, M], F32, kind="ExternalInput")
    wd = nc.dram_tensor("wd", [M, I], F32, kind="ExternalInput")
    b_in = nc.dram_tensor("b_in", [2 * M], F32, kind="ExternalInput")
    b_xp = nc.dram_tensor("b_xp", [S], F32, kind="ExternalInput")
    b_out = nc.dram_tensor("b_out", [M], F32, kind="ExternalInput")
    gamma = nc.dram_tensor("gamma", [M], F32, kind="ExternalInput")
    beta = nc.dram_tensor("beta", [M], F32, kind="ExternalInput")
    alog = nc.dram_tensor("alog", [S], F32, kind="ExternalInput")
    clipx = nc.dram_tensor("clipx", [1], F32, kind="ExternalInput")
    clipd = nc.dram_tensor("clipd", [1], F32, kind="ExternalInput")
    evec = nc.dram_tensor("evec", [B], F32, kind="ExternalInput")
    out = nc.dram_tensor("out", [L, M], F32, kind="ExternalOutput")

    xnfm_h = nc.dram_tensor("xnfm_h", [MP * LT * TL], F32)
    h_h = nc.dram_tensor("h_h", [IP * 128, L], F32)
    winT_bh = nc.dram_tensor("winT_bh", [NP * MP * TL], BF16)
    woutT_bh = nc.dram_tensor("woutT_bh", [MP * MP * TL], BF16)
    wgT_bh = nc.dram_tensor("wgT_bh", [IPAD * MP * TL], BF16)
    wuT_bh = nc.dram_tensor("wuT_bh", [IPAD * MP * TL], BF16)
    wgT_src = nc.dram_tensor("wgT_src", [TPC * MP * TL], BF16)
    wuT_src = nc.dram_tensor("wuT_src", [TPC * MP * TL], BF16)
    sws_src = nc.dram_tensor("sws_src", [128, 2 * TPC], F32)
    sws_dst = nc.dram_tensor("sws_dst", [128 * B, 2 * TPC], F32)
    wdT_bh = nc.dram_tensor("wdT_bh", [MP * IP * TL], BF16)
    arx_src = nc.dram_tensor("arx_src", [128, 1], F32)
    arx_dst = nc.dram_tensor("arx_dst", [128, 1], F32)
    arh_src = nc.dram_tensor("arh_src", [128, 1], F32)
    arh_dst = nc.dram_tensor("arh_dst", [128, 1], F32)
    ag_src = nc.dram_tensor("ag_src", [S, B], F32)
    ag_dst = nc.dram_tensor("ag_dst", [B * S, B], F32)

    rg = [list(range(B))]

    sem_names = [
        "cl", "c0", "wl", "ws", "x1", "v1", "s1", "t1", "e1", "n1", "cc",
        "xa", "va", "sa", "ta", "ea", "wa", "ma", "pa", "sc", "ld", "l2",
        "mc", "ec", "q2", "m2", "e2", "g2", "h2", "m3", "e3", "q3", "o3",
        "bar", "da0", "da1", "db0", "db1", "wa0", "wa1", "pm", "o3a", "o3b",
        "es3", "ev3", "e2v", "wq", "wsv", "da2", "da3", "wqs", "wqg",
        "cg", "wsg", "nsg",
    ]

    with contextlib.ExitStack() as ST:
        def sb(name, shape, dt):
            return ST.enter_context(nc.sbuf_tensor(name, shape, dt))

        def psum(name, shape, dt):
            return ST.enter_context(nc.psum_tensor(name, shape, dt))

        zs = sb("zs", [128, MP * L], BF16)
        xint = sb("xint", [128, max(MP * L, NP * MP * 128,
                                    MP * IP * 128 + MP * TC3 + IP * TC3)], BF16)
        ubuf = sb("ubuf", [S, L], F32)
        af = sb("af", [128, AFC], F32)
        ab = sb("ab", [128, ABC], BF16)
        sc_i32 = sb("sc_i32", [128, 1024], I32)
        c_gam = sb("c_gam", [128, MP], F32)
        c_bet = sb("c_bet", [128, MP], F32)
        c_bin = sb("c_bin", [128, NP], F32)
        c_bout = sb("c_bout", [128, MP], F32)
        c_bxp = sb("c_bxp", [S, 1], F32)
        c_alog = sb("c_alog", [S, 1], F32)
        c_alog_r = sb("c_alog_r", [1, S], F32)
        c_ae = sb("c_ae", [S, 1], F32)
        c_a = sb("c_a", [S, 1], F32)
        c_arep = sb("c_arep", [S, W], F32)
        c_evec = sb("c_evec", [1, B], F32)
        c_coef = sb("c_coef", [S, B], F32)
        c_aer = sb("c_aer", [1, S], F32)
        c_clip = sb("c_clip", [1, 2], F32)
        c_cst = sb("c_cst", [128, 2], F32)
        c_ones = sb("c_ones", [1, 128], F32)
        c_sig = sb("c_sig", [128, 2], F32)
        id_bf = sb("id_bf", [128, 128], BF16)
        id_f32 = sb("id_f32", [128, 128], F32)
        iota_t = sb("iota_t", [128, 128], F32)
        pid_t = sb("pid_t", [128, 1], F32)
        wxp_sb = sb("wxp_sb", [128, MP * S], BF16)
        dm_sb = sb("dm_sb", [S, M], BF16)
        swl = sb("swl", [128, 2 * TPC], F32)
        sws = sb("sws", [128, 16 * TPC], F32)
        esgu = sb("esgu", [128, 16 * TPC], F32)
        swd = sb("swd", [128, MP], F32)
        esd = sb("esd", [128, MP], F32)
        amax = sb("amax", [128, LT + 2], F32)
        hmax = sb("hmax", [128, IP * NCH + 2], F32)
        sclx = sb("sclx", [128, 4], F32)
        sclh = sb("sclh", [128, 4], F32)
        hloc = sb("hloc", [S, B], F32)
        agb = sb("agb", [S, B * B], F32)
        carry = sb("carry", [S, 2 * B], F32)
        mvt = sb("mvt", [128, 16], F32)
        stats = sb("stats", [128, 12 * GN], F32)
        pb = [psum(f"pb{i}", [128, 512], F32) for i in range(4)]
        pt = psum("pt", [128, 4096], BF16)

        semh = {n: ST.enter_context(nc.semaphore("m_" + n)) for n in sem_names}
        block = ST.enter_context(nc.Block())

        v_xa = View(af, 0, M)
        v_xb = View(af, M, M)
        v_xn = View(af, 2 * M, M)
        v_ln2 = View(af, 3 * M, M)
        v_fma = View(af, 3 * M, M)
        v_fmb = View(af, 4 * M, M)
        v_p1s = [View(af, 2048 + 4 * M, TC),
                 View(af, 2048 + 4 * M + TC, TC)]
        v_qga = View(af, 0, 1024)
        v_qgb = View(af, 1024, 1024)
        _qbw = min(1024, L)
        v_qxc = View(af, 2048 + 2 * TC, _qbw)
        v_qxd = View(af, 2048 + 2 * TC + _qbw, _qbw)
        v_fxa = View(af, 2048, M)
        v_fxb = View(af, 2048 + M, M)
        v_fxn = View(af, 2048 + 2 * M, M)
        v_fst = View(af, 2048 + 3 * M, M)
        v_fst2 = View(af, 2048 + 4 * M, M)
        v_x4 = View(af, 2048 + 2 * M, M)
        v_x5 = View(af, 2048 + 3 * M, M)
        v_wd3 = View(af, 4 * TC, I)
        v_xfa = View(af, 2048, TC)
        v_xfb = View(af, 2048 + TC, TC)
        v_hsa = View(af, 0, TC)
        v_hsb = View(af, TC, TC)
        v_gsa = View(af, 2 * TC, TC)
        v_gsb = View(af, 3 * TC, TC)
        v_x3a = View(af, 0, M)
        v_x3b = View(af, M, M)
        v_o3 = View(af, 2 * M, M)
        v_g3 = View(af, 3 * M, TC3)
        v_h3s = [View(af, 3 * M + TC3 + j * 2 * TC3, 2 * TC3)
                 for j in range(4)]
        v_o3b = View(af, 3 * M + 9 * TC3, M)
        v_wxps = View(af, 0, MP * S)
        v_dms = View(af, MP * S, M)
        v_qwa = View(af, 0, I)
        v_qwd = View(af, I, QP)
        v_qgc2 = View(af, 2048, QP)
        b_slA = View(ab, 0, MP * TC)
        b_slB = View(ab, MP * TC, MP * TC)
        b_preA = View(ab, 2 * MP * TC, M)
        b_preB = View(ab, 2 * MP * TC + M, M)
        b_ysz = View(ab, 0, MP * TC)
        b_cwa = View(ab, MP * TC, MP * 128)
        b_cwb = View(ab, MP * TC + MP * 128, MP * 128)
        b_hbf = View(ab, MP * TC + 2 * MP * 128, L)
        b_qsa = View(ab, 0, I)
        b_qsb = View(ab, I, I)
        b_pca = View(ab, 2 * I, MP * 128)
        b_pcb = View(ab, 2 * I + MP * 128, MP * 128)
        b_qev0 = View(ab, 2 * I + 2 * MP * 128, MP * 128)
        b_qev1 = View(ab, 2 * I + 3 * MP * 128, MP * 128)
        b_pcc = View(ab, 2 * I + 4 * MP * 128, MP * 128)
        _wdq_off = 4 * MP * 128
        _qev_off = max(2 * I + 2 * MP * 128, _wdq_off + I)
        b_wdq = View(ab, _wdq_off, I)
        b_qev = View(ab, _qev_off, I)
        b_g2a = View(ab, 0, MP * 128)
        b_g2b = View(ab, MP * 128, MP * 128)
        b_u2a = View(ab, 2 * MP * 128, MP * 128)
        b_u2b = View(ab, 3 * MP * 128, MP * 128)
        b_wdr = View(xint, 0, MP * IP * 128)
        b_t3 = View(xint, MP * IP * 128, MP * TC3)
        b_hqs = [View(ab, 0, IP * TC3), View(ab, IP * TC3, IP * TC3),
                 View(xint, MP * IP * 128 + MP * TC3, IP * TC3)]

        def emit(which, eng):
            sy = eng if which == "sync" else _FakeEngine()
            ve = eng if which == "vector" else _FakeEngine()
            sc = eng if which == "scalar" else _FakeEngine()
            gp = eng if which == "gpsimd" else _FakeEngine()
            te = eng if which == "tensor" else _FakeEngine()
            real = eng

            sm = {n: Sem(h) for n, h in semh.items()}
            dma_sems_sync = [sm[n] for n in
                             ("cl", "wl", "ws", "x1", "n1", "xa", "wa", "ld",
                              "l2", "o3", "o3a", "o3b", "da0", "da1",
                              "da2", "da3", "db0", "db1", "wa0", "wa1",
                              "wsg", "nsg")]
            bar_target = [0]

            def barrier():
                if which == "sync":
                    for s_ in dma_sems_sync:
                        s_.wait(real)
                if which == "scalar":
                    sm["h2"].wait(real)
                real.drain().then_inc(sm["bar"].h, 1)
                bar_target[0] += 5
                real.wait_ge(sm["bar"].h, bar_target[0])

            cl, c0 = sm["cl"], sm["c0"]
            cl.inc(sy.dma_start(c_gam[:, :], AP(gamma, 0, [[1, 128], [128, MP]]),
                     allow_slow_non_contiguous=True), 16)
            cl.inc(sy.dma_start(c_bet[:, :], AP(beta, 0, [[1, 128], [128, MP]]),
                     allow_slow_non_contiguous=True), 16)
            cl.inc(sy.dma_start(c_bin[:, :], AP(b_in, 0, [[1, 128], [128, NP]]),
                     allow_slow_non_contiguous=True), 16)
            cl.inc(sy.dma_start(c_bout[:, :], AP(b_out, 0, [[1, 128], [128, MP]]),
                     allow_slow_non_contiguous=True), 16)
            cl.inc(sy.dma_start(c_bxp[:, :], AP(b_xp, 0, [[1, S], [1, 1]])), 16)
            cl.inc(sy.dma_start(c_alog[:, :], AP(alog, 0, [[1, S], [1, 1]])), 16)
            cl.inc(sy.dma_start(c_alog_r[:, :], AP(alog, 0, [[S, 1], [1, S]])), 16)
            cl.inc(sy.dma_start(c_evec[:, :], AP(evec, 0, [[B, 1], [1, B]])), 16)
            cl.inc(sy.dma_start(c_clip[:, 0:1], AP(clipx, 0, [[1, 1], [1, 1]])), 16)
            cl.inc(sy.dma_start(c_clip[:, 1:2], AP(clipd, 0, [[1, 1], [1, 1]])), 16)
            cl.inc(sy.dma_start(
                v_wxps[:, 0:MP * S],
                AP(wxpT, 0, [[S, 128], [128 * S, MP], [1, S]])), 16)
            cl.inc(sy.dma_start(v_dms[0:S, 0:M], dmat[:, :]), 16)

            gp.iota(iota_t[:, :], [[1, 128]], channel_multiplier=0,
                    allow_small_or_imprecise_dtypes=True)
            gp.iota(pid_t[:, :], [[0, 1]], channel_multiplier=1,
                    allow_small_or_imprecise_dtypes=True)
            c0.inc(gp.drain(), 1)
            c0.wait(ve, 1)
            ve.tensor_scalar(id_bf[:, :], iota_t[:, :], pid_t[:, 0:1], None,
                             ALU.is_equal)
            ve.tensor_scalar(id_f32[:, :], iota_t[:, :], pid_t[:, 0:1], None,
                             ALU.is_equal)
            cl.wait(ve)
            ve.tensor_copy(wxp_sb[:, :], v_wxps[:, 0:MP * S])
            ve.tensor_copy(dm_sb[:, :], v_dms[0:S, 0:M])
            ve.memset(c_cst[:, 0:1], 1e-5)
            ve.memset(c_cst[:, 1:2], -1.0)
            ve.memset(c_ones[0:1, :], 1.0)
            c0.inc(ve.drain(), 1)

            cl.wait(sc)
            c0.wait(sc, 2)
            sc.activation(c_sig[0:1, 0:2], c_clip[0:1, 0:2], AF.Sigmoid)
            sc.activation(c_ae[:, 0:1], c_alog[:, 0:1], AF.Exp)
            sc.activation(c_aer[0:1, :], c_alog_r[0:1, :], AF.Exp)
            sc.drain()
            sc.activation(c_a[:, 0:1], c_ae[:, 0:1], AF.Exp, scale=c_cst[0:S, 1:2])
            c0.inc(sc.drain(), 1)
            c0.wait(ve, 3)
            ve.memset(c_arep[:, :], 1.0)
            ve.drain()
            ve.tensor_scalar_mul(c_arep[:, :], c_arep[:, :], c_a[:, 0:1])
            c0.inc(ve.drain(), 1)
            c0.wait(te, 3)
            te.matmul(pb[2][:, 0:2], c_ones[0:1, 0:128], c_sig[0:1, 0:2],
                      start=True, stop=True)
            c0.inc(te.drain(), 1)
            c0.wait(ve, 5)
            ve.tensor_copy(c_sig[:, 0:2], pb[2][:, 0:2])
            c0.inc(ve.drain(), 1)
            c0.wait(te, 3)
            te.matmul(pb[3][0:S, 0:B], c_aer[0:1, 0:S], c_evec[0:1, 0:B],
                      start=True, stop=True)
            c0.inc(te.drain(), 1)
            c0.wait(sc, c0.n)
            sc.activation(c_coef[:, :], pb[3][0:S, 0:B], AF.Exp, scale=c_cst[0:S, 1:2])
            c0.inc(sc.drain(), 1)

            wl, ws, e1 = sm["wl"], sm["ws"], sm["e1"]
            t1, ea = sm["t1"], sm["ea"]
            x1, v1, s1, n1 = sm["x1"], sm["v1"], sm["s1"], sm["n1"]
            wq, wsv = sm["wq"], sm["wsv"]
            wqs, wqg = sm["wqs"], sm["wqg"]
            cg, wsg, nsg = sm["cg"], sm["wsg"], sm["nsg"]
            IMG = MP * 128
            r_qld = Res(2)
            r_qi32 = Res(2)
            r_qst = Res(2)
            r_qpt = Res(2)
            r_qev = Res(2, dma=True)
            r_pcs = Res(3, dma=True)
            r_fx = Res(2)
            r_fxn = Res(1)
            r_ftp = Res(1)
            r_fst = Res(1, dma=True)
            c0.wait(sy, 2)

            def w_load(ui, kind, r):
                sl = ui % 2
                buf = (v_qga, v_qgb)[sl]
                r_qld.acq(sy, ui)
                da = sm[("da0", "da1")[sl]]
                if kind in ("wg", "wu"):
                    wsrc = wg_my if kind == "wg" else wu_my
                    da.inc(sy.dma_start(buf[:, 0:M],
                                        wsrc[r * 128:(r + 1) * 128, :]), 16)
                elif kind == "pcw":
                    k, half = r // 2, r % 2
                    da.inc(sy.dma_start(
                        buf[:, 0:IMG],
                        AP(winT, k * 128 * 2 * M + half * M,
                           [[2 * M, 128], [1, M]])), 16)
                else:
                    da.inc(sy.dma_start(
                        buf[:, 0:IMG],
                        AP(woutT, r * 128,
                           [[M, 128], [128 * M, MP], [1, 128]])), 16)

            pcq2 = [0]

            def wq_s1(ui, wsrc, r):
                sl = ui % 2
                buf = (v_qga, v_qgb)[sl]
                da = sm[("da0", "da1")[sl]]
                da.wait(ve)
                wsv.wait(ve)
                wq.inc(ve.tensor_reduce(mvt[:, sl:sl + 1], buf[:, 0:M], AX.X,
                                        ALU.max, apply_absolute_value=True), 1)
                wq.wait(ve)
                wq.inc(ve.reciprocal(mvt[:, 2 + sl:3 + sl], mvt[:, sl:sl + 1]), 1)

            def wq_s2(ui, svec, r, dst):
                sl = ui % 2
                buf = (v_qga, v_qgb)[sl]
                stg = (b_qsa, b_qsb)[sl]
                wq.wait(sc)
                wsv.inc(sc.mul(swl[:, svec + r:svec + r + 1],
                               mvt[:, sl:sl + 1], 1.0 / 127.0), 1)
                PH = min(512, M)
                r_qst.acq(gp, ui)
                wq.wait(ve)
                for j in range(-(-M // PH)):
                    cw = min(PH, M - j * PH)
                    pc = pcq2[0]
                    pcq2[0] += 1
                    io = (pc % 2) * 512
                    r_qi32.acq(ve, pc)
                    wq.inc(ve.tensor_scalar(sc_i32[:, io:io + cw],
                                            buf[:, j * PH:j * PH + cw],
                                            mvt[:, 2 + sl:3 + sl], 127.0,
                                            ALU.mult, ALU.mult), 1)
                    wq.wait(gp)
                    wqg.inc(gp.tensor_scalar(stg[:, j * PH:j * PH + cw],
                                             sc_i32[:, io:io + cw],
                                             127.0, -128.0, ALU.min, ALU.max), 1)
                    r_qi32.rel(pc, wqg)
                r_qld.rel(ui, wq)
                wqg.wait(te)
                r_qpt.acq(te, ui)
                for k in range(MP):
                    inst = te.transpose(
                        pt[:, sl * IMG + k * 128:sl * IMG + (k + 1) * 128],
                        stg[:, k * 128:(k + 1) * 128], id_bf[:, :])
                t1.inc(inst, 1)
                r_qst.rel(ui, t1)
                t1.wait(sc)
                ev = (b_qev0, b_qev1)[sl]
                r_qev.acq(sc, ui)
                inst = sc.activation(ev[:, 0:IMG], pt[:, sl * IMG:(sl + 1) * IMG],
                                     AF.Copy)
                ea.inc(inst, 1)
                r_qpt.rel(ui, ea)
                ea.wait(sy)
                wsg.inc(sy.dma_start(
                    AP(dst, r * 128 * IMG, [[IMG, 128], [1, IMG]]),
                    ev[:, 0:IMG]), 16)
                r_qev.rel(ui, wsg)

            def pc_unit(ui, kind, n):
                sl = ui % 2
                buf = (v_qga, v_qgb)[sl]
                stg = (b_pca, b_pcb, b_pcc)[ui % 3]
                da = sm[("da0", "da1")[sl]]
                da.wait(ve)
                r_pcs.acq(ve, ui)
                wq.inc(ve.tensor_copy(stg[:, 0:IMG], buf[:, 0:IMG]), 1)
                r_qld.rel(ui, wq)
                wq.wait(sy)
                if kind == "pcw":
                    k, half = n // 2, n % 2
                    dap = AP(winT_bh, k * 128 * 2 * M + half * M,
                             [[2 * M, 128], [1, M]])
                else:
                    dap = AP(woutT_bh, n * 128 * IMG, [[IMG, 128], [1, IMG]])
                ws.inc(sy.dma_start(dap, stg[:, 0:IMG]), 16)
                r_pcs.rel(ui, ws)

            def p1_load(t):
                sl = t % 2
                buf = (v_fxa, v_fxb)[sl]
                r_fx.acq(sy, t)
                db = sm[("db0", "db1")[sl]]
                db.inc(sy.dma_start(buf[:, 0:M], xb[t * 128:(t + 1) * 128, :]), 16)

            def p1_tile(t):
                sl = t % 2
                buf = (v_fxa, v_fxb)[sl]
                db = sm[("db0", "db1")[sl]]
                db.wait(ve)
                for g in range(GN):
                    inst = ve.bn_stats(stats[:, 6 * g:6 * g + 6],
                                       buf[:, g * GSZ:min(M, (g + 1) * GSZ)])
                v1.inc(inst, 1)
                v1.wait(ve)
                s1.wait(ve)
                inst = ve.bn_aggr(mvt[:, 4:6], stats[:, 0:6 * GN])
                v1.inc(inst, 1)
                v1.wait(sc)
                inst = sc.activation(mvt[:, 6:7], mvt[:, 5:6], AF.Sqrt,
                                     bias=c_cst[:, 0:1])
                s1.inc(inst, 1)
                return buf

            def p1_tile_fin(t, buf):
                s1.wait(ve)
                inst = ve.reciprocal(mvt[:, 7:8], mvt[:, 6:7])
                v1.inc(inst, 1)
                v1.wait(ve)
                r_fxn.acq(ve, 0)
                inst = ve.tensor_scalar(v_fxn[:, 0:M], buf[:, 0:M], mvt[:, 4:5],
                                        mvt[:, 7:8], ALU.subtract, ALU.mult)
                v1.inc(inst, 1)
                r_fx.rel(t, v1)
                v1.wait(ve)
                inst = ve.tensor_reduce(amax[:, t:t + 1], v_fxn[:, 0:M], AX.X,
                                        ALU.max, apply_absolute_value=True)
                v1.inc(inst, 1)
                v1.wait(te)
                r_ftp.acq(te, 0)
                for k in range(MP):
                    bank = pb[0] if k < 4 else pb[1]
                    inst = te.transpose(bank[:, (k % 4) * 128:(k % 4 + 1) * 128],
                                        v_fxn[:, k * 128:(k + 1) * 128],
                                        id_f32[:, :])
                t1.inc(inst, 1)
                r_fxn.rel(0, t1)
                t1.wait(sc)
                tl = t % 2
                if tl == 0:
                    r_fst.acq(sc, 0)
                inst = sc.copy(
                    AP(af, v_fst.off + tl * 128,
                       [[AFC, 128], [256, min(4, MP)], [1, 128]]),
                    pb[0][:, 0:min(512, M)])
                if MP > 4:
                    inst = sc.copy(
                        AP(af, v_fst.off + 1024 + tl * 128,
                           [[AFC, 128], [256, 4], [1, 128]]),
                        pb[1][:, 0:M - 512])
                e1.inc(inst, 1)
                r_ftp.rel(0, e1)
                if tl == 1 or t == LT - 1:
                    e1.wait(sy)
                    t0 = t - tl
                    n1.inc(sy.dma_start(
                        AP(xnfm_h, t0 * 128,
                           [[L, 128], [128 * L, MP], [1, (tl + 1) * 128]]),
                        AP(af, v_fst.off,
                           [[AFC, 128], [256, MP], [1, (tl + 1) * 128]])), 16)
                    r_fst.rel(0, n1)

            wunits = ([("wg", r) for r in range(TPC)] +
                      [("wu", r) for r in range(TPC)] +
                      [("pcw", n) for n in range(NP)] +
                      [("pco", n) for n in range(MP)])

            def emit_wunit(i, kind, r):
                if kind == "wg":
                    wq_s1(i, wg_my, r)
                    wq_s2(i, 0, r, wgT_src)
                elif kind == "wu":
                    wq_s1(i, wu_my, r)
                    wq_s2(i, TPC, r, wuT_src)
                if i == 2 * TPC - 1:
                    wsv.wait(sy)
                    nsg.inc(sy.dma_start(sws_src[:, :], swl[:, 0:2 * TPC]), 16)
                    wsg.wait(gp)
                    nsg.wait(gp)
                    cg.inc(gp.collective_compute(
                        "AllGather", ALU.bypass, replica_groups=rg,
                        ins=[sws_src.ap().opt()], outs=[sws_dst.ap().opt()]), 1)
                    cg.inc(gp.collective_compute(
                        "AllGather", ALU.bypass, replica_groups=rg,
                        ins=[wgT_src.ap().opt()], outs=[wgT_bh.ap().opt()]), 1)
                    cg.inc(gp.collective_compute(
                        "AllGather", ALU.bypass, replica_groups=rg,
                        ins=[wuT_src.ap().opt()], outs=[wuT_bh.ap().opt()]), 1)
                else:
                    pc_unit(i, kind, r)
                if i + 2 < len(wunits):
                    w_load(i + 2, *wunits[i + 2])

            for _w in range(min(2, len(wunits))):
                w_load(_w, *wunits[_w])
            p1_load(0)
            if LT > 1:
                p1_load(1)
            wi = 0
            for t in range(LT):
                buf = p1_tile(t)
                for _ in range(2):
                    if wi < len(wunits):
                        emit_wunit(wi, *wunits[wi])
                        wi += 1
                p1_tile_fin(t, buf)
                if t + 2 < LT:
                    p1_load(t + 2)
                if t % 4 == 3 and wi < len(wunits):
                    emit_wunit(wi, *wunits[wi])
                    wi += 1
            while wi < len(wunits):
                emit_wunit(wi, *wunits[wi])
                wi += 1

            v1.wait(ve)
            inst = ve.tensor_reduce(amax[:, LT:LT + 1], amax[:, 0:LT], AX.X,
                                    ALU.max)
            v1.inc(inst, 1)
            v1.wait(sy)
            n1.inc(sy.dma_start(arx_src[:, :], amax[:, LT:LT + 1]), 16)
            cc = sm["cc"]
            n1.wait(gp)
            cc.inc(gp.collective_compute(
                "AllReduce", ALU.max, replica_groups=rg,
                ins=[arx_src.ap().opt()], outs=[arx_dst.ap().opt()]), 1)

            barrier()

            xa, va, sa = sm["xa"], sm["va"], sm["sa"]
            scn = sm["sc"]
            ta, wa, ma, pa = sm["ta"], sm["wa"], sm["ma"], sm["pa"]
            r_xt = Res(4)
            r_pre = Res(2)
            r_apt = Res(1)
            r_sl = Res(2)
            r_bk = Res(3)
            r_u = Res(1)
            NPAIR = -(-CT // 2) * NCH

            def pa_load(gp_):
                if gp_ >= NPAIR:
                    return
                tt0 = 2 * (gp_ % (-(-CT // 2)))
                t0 = (gp_ // (-(-CT // 2))) * CT + tt0
                pair = tt0 + 1 < CT
                xs0, xs1 = (v_xa, v_xb) if gp_ % 2 == 0 else (v_x4, v_x5)
                das = (sm["da0"], sm["da1"]) if gp_ % 2 == 0 else \
                      (sm["da2"], sm["da3"])
                r_xt.acq(sy, 2 * gp_)
                das[0].inc(sy.dma_start(xs0[:, 0:M],
                                        xt[t0 * 128:(t0 + 1) * 128, :]), 16)
                if pair:
                    r_xt.acq(sy, 2 * gp_ + 1)
                    das[1].inc(sy.dma_start(
                        xs1[:, 0:M], xt[(t0 + 1) * 128:(t0 + 2) * 128, :]), 16)

            PAIRS = -(-CT // 2)

            def ln_stage(src0, src1, pair, q8):
                va.wait(ve)
                sa.wait(ve)
                for g in range(GN):
                    inst = ve.bn_stats(stats[:, 6 * g:6 * g + 6],
                                       src0[:, g * GSZ:min(M, (g + 1) * GSZ)])
                if pair:
                    for g in range(GN):
                        inst = ve.bn_stats(
                            stats[:, 6 * GN + 6 * g:6 * GN + 6 * g + 6],
                            src1[:, g * GSZ:min(M, (g + 1) * GSZ)])
                va.inc(inst, 1)
                va.wait(ve)
                inst = ve.bn_aggr(mvt[:, q8:q8 + 2], stats[:, 0:6 * GN])
                if pair:
                    inst = ve.bn_aggr(mvt[:, q8 + 2:q8 + 4],
                                      stats[:, 6 * GN:12 * GN])
                va.inc(inst, 1)

            def st_sqrt(pair, q8):
                va.wait(sc)
                if pair:
                    inst = sc.activation(mvt[:, q8 + 4:q8 + 6],
                                         mvt[:, q8 + 1:q8 + 4:2],
                                         AF.Sqrt, bias=c_cst[:, 0:1])
                else:
                    inst = sc.activation(mvt[:, q8 + 4:q8 + 5],
                                         mvt[:, q8 + 1:q8 + 2],
                                         AF.Sqrt, bias=c_cst[:, 0:1])
                sa.inc(inst, 1)

            def ln_apply(src0, src1, dst0, dst1, pair, q8):
                sa.wait(ve)
                w = 2 if pair else 1
                inst = ve.reciprocal(mvt[:, q8 + 6:q8 + 6 + w],
                                     mvt[:, q8 + 4:q8 + 4 + w])
                va.inc(inst, 1)
                va.wait(ve)
                inst = ve.tensor_scalar(dst0[:, 0:M], src0[:, 0:M],
                                        mvt[:, q8:q8 + 1],
                                        mvt[:, q8 + 6:q8 + 7],
                                        ALU.subtract, ALU.mult)
                if pair:
                    inst = ve.tensor_scalar(dst1[:, 0:M], src1[:, 0:M],
                                            mvt[:, q8 + 2:q8 + 3],
                                            mvt[:, q8 + 7:q8 + 8],
                                            ALU.subtract, ALU.mult)
                va.inc(inst, 1)

            def ln_steps(ch):
                steps = []
                nrm_v = (b_slA, b_slB)[ch % 2]
                for pp in range(PAIRS):
                    gp_ = ch * PAIRS + pp
                    tt0 = 2 * pp
                    pair = tt0 + 1 < CT
                    q8 = (gp_ % 2) * 8
                    xs0, xs1 = (v_xa, v_xb) if gp_ % 2 == 0 else (v_x4, v_x5)
                    das = (sm["da0"], sm["da1"]) if gp_ % 2 == 0 else \
                          (sm["da2"], sm["da3"])

                    def s_ln1(gp_=gp_, pair=pair, q8=q8, xs0=xs0, xs1=xs1,
                              das=das):
                        das[0].wait(ve)
                        if pair:
                            das[1].wait(ve)
                        ln_stage(xs0, xs1, pair, q8)
                        pa_load(gp_ + 2)
                    steps.append(s_ln1)
                    steps.append(lambda pair=pair, q8=q8: st_sqrt(pair, q8))

                    def s_ap1(gp_=gp_, pair=pair, q8=q8, xs0=xs0, xs1=xs1):
                        ln_apply(xs0, xs1, v_xn, v_ln2, pair, q8)
                        r_xt.rel(2 * gp_, va)
                        if pair:
                            r_xt.rel(2 * gp_ + 1, va)
                        ln_stage(v_xn, v_ln2, pair, q8)
                    steps.append(s_ap1)
                    steps.append(lambda pair=pair, q8=q8: st_sqrt(pair, q8))

                    def s_ap2(pair=pair, q8=q8):
                        r_pre.acq(ve, 0)
                        if pair:
                            r_pre.acq(ve, 1)
                        ln_apply(v_xn, v_ln2, b_preA, b_preB, pair, q8)
                    steps.append(s_ap2)

                    def s_tr(pp=pp, tt0=tt0, pair=pair):
                        va.wait(te)
                        if pp == 0:
                            r_apt.acq(te, 0)
                        for k in range(MP):
                            inst = te.transpose(
                                pt[:, k * TC + tt0 * 128:
                                   k * TC + (tt0 + 1) * 128],
                                b_preA[:, k * 128:(k + 1) * 128], id_bf[:, :])
                        if pair:
                            for k in range(MP):
                                inst = te.transpose(
                                    pt[:, k * TC + (tt0 + 1) * 128:
                                       k * TC + (tt0 + 2) * 128],
                                    b_preB[:, k * 128:(k + 1) * 128],
                                    id_bf[:, :])
                        ta.inc(inst, 1)
                        r_pre.rel(0, ta)
                        if pair:
                            r_pre.rel(1, ta)
                    steps.append(s_tr)

                def s_gb(ch=ch, nrm_v=nrm_v):
                    ta.wait(ve)
                    r_sl.acq(ve, ch)
                    for k in range(MP):
                        inst = ve.tensor_scalar(nrm_v[:, k * TC:(k + 1) * TC],
                                                pt[:, k * TC:(k + 1) * TC],
                                                c_gam[:, k:k + 1],
                                                c_bet[:, k:k + 1],
                                                ALU.mult, ALU.add)
                    ea.inc(inst, 1)
                    r_apt.rel(0, ea)
                steps.append(s_gb)
                return steps

            pa_load(0)
            pa_load(1)
            pa_q = []
            pa_q.extend(ln_steps(0))
            while pa_q:
                pa_q.pop(0)()
            ws.wait(sy)
            was = sm["wa0"]
            was.inc(sy.dma_start(
                View(xint, 0, NP * MP * 128)[:, 0:NP * MP * 128],
                AP(winT_bh, 0, [[2 * M, 128], [128 * 2 * M, MP],
                                [1, 2 * M]])), 16)
            was.wait(te)
            for ch in range(NCH):
                nrm_v = (b_slA, b_slB)[ch % 2]
                xp_v = (b_slA, b_slB)[(ch + 1) % 2]
                if ch + 1 < NCH:
                    pa_q.extend(ln_steps(ch + 1))
                r_sl.acq(sc, ch + 1)
                for n in range(NP):
                    if pa_q:
                        pa_q.pop(0)()
                    if n == MP:
                        pa.wait(te)
                        r_u.acq(te, 0)
                        for k in range(MP):
                            inst = te.matmul(pb[3][0:S, 0:TC],
                                             wxp_sb[:, k * S:(k + 1) * S],
                                             xp_v[:, k * TC:(k + 1) * TC],
                                             start=(k == 0), stop=(k == MP - 1))
                        ma.inc(inst, 1)
                        r_sl.rel(ch + 1, ma)
                        ma.wait(ve)
                        inst = ve.tensor_scalar(ubuf[0:S, ch * TC:(ch + 1) * TC],
                                                pb[3][0:S, 0:TC],
                                                c_bxp[0:S, 0:1], None, ALU.add)
                        va.inc(inst, 1)
                        r_u.rel(0, va)
                        va.wait(ve)
                        if ch > 0:
                            scn.wait(ve)
                        p1o = v_p1s[(ch + 1) % 2]
                        p1n = v_p1s[ch % 2]
                        for b in range(B):
                            init = 0.0 if ch == 0 else \
                                p1o[0:S, TC - B + b:TC - B + b + 1]
                            inst = ve.tensor_tensor_scan(
                                p1n[0:S, b:TC:B], c_arep[0:S, 0:TC // B],
                                ubuf[0:S, ch * TC + b:(ch + 1) * TC:B],
                                init, ALU.mult, ALU.add)
                        scn.inc(inst, 1)
                    wtile = View(xint, 0, NP * MP * 128)
                    ea.wait(te)
                    bank = pb[n % 3]
                    r_bk.acq(te, n)
                    for k in range(MP):
                        co = k * 2 * M + n * 128
                        inst = te.matmul(bank[:, 0:TC],
                                         wtile[:, co:co + 128],
                                         nrm_v[:, k * TC:(k + 1) * TC],
                                         start=(k == 0), stop=(k == MP - 1))
                    ma.inc(inst, 1)
                    if n == NP - 1:
                        r_sl.rel(ch, ma)
                    ma.wait(sc)
                    if n < MP:
                        inst = sc.activation(xp_v[:, n * TC:(n + 1) * TC],
                                             bank[:, 0:TC],
                                             AF.Silu, bias=c_bin[:, n:n + 1])
                    else:
                        inst = sc.activation(zs[:, (n - MP) * L + ch * TC:
                                                 (n - MP) * L + (ch + 1) * TC],
                                             bank[:, 0:TC],
                                             AF.Sigmoid, bias=c_bin[:, n:n + 1])
                    pa.inc(inst, 1)
                    r_bk.rel(n, pa)
                while pa_q:
                    pa_q.pop(0)()

            scn.wait(ve)
            inst = ve.tensor_copy(hloc[0:S, 0:B],
                                  v_p1s[(NCH - 1) % 2][0:S, TC - B:TC])
            scn.inc(inst, 1)
            scn.wait(sy)
            n1.inc(sy.dma_start(ag_src[:, :], hloc[0:S, 0:B]), 16)
            n1.wait(gp)
            cc.inc(gp.collective_compute(
                "AllGather", ALU.bypass, replica_groups=rg,
                ins=[ag_src.ap().opt()], outs=[ag_dst.ap().opt()]), 1)

            barrier()

            ld = sm["ld"]
            cc.wait(sy, 2)
            ld.inc(sy.dma_start(agb[0:S, 0:B * B],
                                AP(ag_dst, 0, [[B, S], [S * B, B], [1, B]])), 16)
            ld.wait(ve)
            inst = ve.tensor_scalar_mul(carry[0:S, 0:B], agb[0:S, 0:B],
                                        c_coef[0:S, 0:1])
            for k in range(1, B):
                scn.inc(inst, 1)
                scn.wait(ve)
                src = carry[0:S, 0:B] if k % 2 == 1 else carry[0:S, B:2 * B]
                dst = carry[0:S, B:2 * B] if k % 2 == 1 else carry[0:S, 0:B]
                inst = ve.scalar_tensor_tensor(dst, agb[0:S, k * B:(k + 1) * B],
                                               c_coef[0:S, k:k + 1], src,
                                               ALU.mult, ALU.add)
            cfin = carry[0:S, B:2 * B] if (B - 1) % 2 == 1 else carry[0:S, 0:B]
            scn.inc(inst, 1)

            q2, g2 = sm["q2"], sm["g2"]
            cc.wait(sy, 1)
            scn.wait(sy)
            ld.inc(sy.dma_start(sclx[:, 0:1], arx_dst[:, :]), 16)
            pm = sm["pm"]
            ld.wait(te)
            inst = te.transpose(pb[3][0:1, 0:128], sclx[:, 0:1], id_f32[:, :])
            pm.inc(inst, 1)
            pm.wait(ve)
            inst = ve.reduce_max(sclx[0:1, 0:1], pb[3][0:1, 0:128], AX.X)
            pm.inc(inst, 1)
            pm.wait(te)
            inst = te.matmul(pb[2][:, 0:1], c_ones[0:1, 0:128], sclx[0:1, 0:1],
                             start=True, stop=True)
            pm.inc(inst, 1)
            pm.wait(ve)
            inst = ve.tensor_copy(sclx[:, 1:2], pb[2][:, 0:1])
            pm.inc(inst, 1)
            pm.wait(ve)
            inst = ve.scalar_tensor_tensor(sclx[:, 2:3], c_sig[:, 0:1],
                                           1.0 / 127.0, sclx[:, 1:2],
                                           ALU.mult, ALU.mult)
            pm.inc(inst, 1)
            pm.wait(ve)
            inst = ve.reciprocal(sclx[:, 3:4], sclx[:, 2:3])
            pm.inc(inst, 1)
            pm.wait(ve)
            sm["cg"].wait(sy, 1)
            sm["nsg"].inc(sy.dma_start(
                sws[:, 0:16 * TPC],
                AP(sws_dst, 0, [[2 * TPC, 128], [128 * 2 * TPC, B],
                                [1, 2 * TPC]])), 16)
            sm["nsg"].wait(ve)
            inst = ve.tensor_scalar_mul(esgu[:, 0:16 * TPC], sws[:, 0:16 * TPC],
                                        sclx[:, 2:3])
            q2.inc(inst, 1)

            HF = min(1024, L)
            r_q2 = Res(4)
            r_qi2 = Res(2)
            pcq = [0]
            q2_q = []

            def q2_steps(k, hf):
                idx = k * (L // HF) + hf
                buf = (v_qga, v_qgb, v_qxc, v_qxd)[idx % 4]

                def s_load():
                    r_q2.acq(sy, idx)
                    da = sm[("da0", "da1", "da2", "da3")[idx % 4]]
                    da.inc(sy.dma_start(
                        buf[:, 0:HF],
                        AP(xnfm_h, k * 128 * L + hf * HF,
                           [[L, 128], [1, HF]])), 16)
                steps = [s_load]

                def mk_piece(j):
                    def f():
                        cw = min(QP, HF - j * QP)
                        off = k * L + hf * HF + j * QP
                        if j == 0:
                            da = sm[("da0", "da1", "da2", "da3")[idx % 4]]
                            da.wait(ve)
                        pc = pcq[0]
                        pcq[0] += 1
                        io = (pc % 2) * 512
                        r_qi2.acq(ve, pc)
                        inst = ve.tensor_scalar(sc_i32[:, io:io + cw],
                                                buf[:, j * QP:j * QP + cw],
                                                sclx[:, 3:4], None, ALU.mult)
                        q2.inc(inst, 1)
                        if j == -(-HF // QP) - 1:
                            r_q2.rel(idx, q2)
                        q2.wait(gp)
                        inst = gp.tensor_scalar(xint[:, off:off + cw],
                                                sc_i32[:, io:io + cw],
                                                127.0, -128.0,
                                                ALU.min, ALU.max)
                        wqg.inc(inst, 1)
                        r_qi2.rel(pc, wqg)
                    return f
                for j in range(-(-HF // QP)):
                    steps.append(mk_piece(j))
                return steps

            q2_lq = []
            for k in range(MP):
                for hf in range(L // HF):
                    st = q2_steps(k, hf)
                    q2_lq.append(st[0])
                    q2_q.extend(st[1:])
            NPC = -(-HF // QP)
            q2_st = [0, 0]

            def q2_tick():
                if q2_lq and (q2_st[0] - q2_st[1] // NPC) < 4:
                    q2_lq.pop(0)()
                    q2_st[0] += 1
                elif q2_q:
                    q2_q.pop(0)()
                    q2_st[1] += 1

            mc, ec, l2 = sm["mc"], sm["ec"], sm["l2"]
            r_bkc = Res(3)
            r_ysz = Res(1)
            r_wsc = Res(2)
            r_xfc = Res(2)
            WSUB = TC // B

            def pc_scan(ch):
                for b in range(B):
                    init = cfin[:, b:b + 1] if ch == 0 else \
                        b_hbf[0:S, ch * TC - B + b:ch * TC - B + b + 1]
                    if b == 0:
                        scn.wait(ve)
                    inst = ve.tensor_tensor_scan(
                        b_hbf[0:S, ch * TC + b:(ch + 1) * TC:B],
                        c_arep[0:S, 0:WSUB],
                        ubuf[0:S, ch * TC + b:(ch + 1) * TC:B],
                        init, ALU.mult, ALU.add)
                scn.inc(inst, 1)

            pc_scan(0)
            for ch in range(NCH):
                r_ysz.acq(ve, 0)
                for m in range(MP):
                    bank = pb[m % 3]
                    r_bkc.acq(te, m)
                    if m == 0:
                        scn.wait(te)
                        if ch == 0:
                            pm.wait(te)
                    inst = te.matmul(bank[:, 0:TC],
                                     dm_sb[0:S, m * 128:(m + 1) * 128],
                                     b_hbf[0:S, ch * TC:(ch + 1) * TC],
                                     start=True, stop=True)
                    mc.inc(inst, 1)
                    mc.wait(ve)
                    inst = ve.tensor_tensor(
                        b_ysz[:, m * TC:(m + 1) * TC], bank[:, 0:TC],
                        zs[:, m * L + ch * TC:m * L + (ch + 1) * TC],
                        ALU.mult)
                    ec.inc(inst, 1)
                    r_bkc.rel(m, ec)
                    q2_tick()
                if ch + 1 < NCH:
                    pc_scan(ch + 1)
                for mo in range(MP):
                    widx = ch * MP + mo
                    wbuf = (b_cwa, b_cwb)[widx % 2]
                    r_wsc.acq(sy, widx)
                    if ch == 0 and mo == 0:
                        ws.wait(sy)
                    was = sm[("wa0", "wa1")[widx % 2]]
                    was.inc(sy.dma_start(
                        wbuf[:, 0:MP * 128],
                        AP(woutT_bh, mo * 128 * MP * 128,
                           [[MP * 128, 128], [1, MP * 128]])), 16)
                    xslot = (v_xfa, v_xfb)[widx % 2]
                    r_xfc.acq(sy, widx)
                    dbm = sm[("db0", "db1")[widx % 2]]
                    dbm.inc(sy.dma_start(
                        xslot[:, 0:TC],
                        AP(xnfm_h, mo * 128 * L + ch * TC,
                           [[L, 128], [1, TC]])), 16)
                    was.wait(te)
                    if mo == 0:
                        ec.wait(te)
                    bank = pb[mo % 3]
                    r_bkc.acq(te, mo)
                    for k in range(MP):
                        inst = te.matmul(bank[:, 0:TC],
                                         wbuf[:, k * 128:(k + 1) * 128],
                                         b_ysz[:, k * TC:(k + 1) * TC],
                                         start=(k == 0), stop=(k == MP - 1))
                    mc.inc(inst, 1)
                    r_wsc.rel(widx, mc)
                    if mo == MP - 1:
                        r_ysz.rel(0, mc)
                    mc.wait(ve)
                    dbm.wait(ve)
                    inst = ve.scalar_tensor_tensor(
                        zs[:, mo * L + ch * TC:mo * L + (ch + 1) * TC],
                        bank[:, 0:TC], c_bout[:, mo:mo + 1],
                        xslot[:, 0:TC], ALU.add, ALU.add)
                    ec.inc(inst, 1)
                    r_bkc.rel(mo, ec)
                    r_xfc.rel(widx, ec)
                    q2_tick()
            while q2_q or q2_lq:
                q2_tick()

            barrier()

            m2, e2, h2 = sm["m2"], sm["e2"], sm["h2"]
            e2v = sm["e2v"]
            wq, wsv = sm["wq"], sm["wsv"]
            wqs, wqg = sm["wqs"], sm["wqg"]
            t1, ea = sm["t1"], sm["ea"]
            r_wg2 = Res(2)
            r_wu2 = Res(2)
            r_gs = Res(2)
            r_hb = Res(2, dma=True)
            r_bkg = Res(2)
            r_bku = Res(2)
            r_wdl = Res(1)
            r_wdq = Res(1)
            r_wdpt = Res(1)
            r_wdev = Res(1, dma=True)
            r_wi32 = Res(2)

            QW = 512
            NQ = -(-I // QW)

            def wd_steps(mh):
                steps = []

                def s_load():
                    r_wdl.acq(sy, 0)
                    wl.inc(sy.dma_start(v_wd3[:, 0:I],
                                        wd[mh * 128:(mh + 1) * 128, :]), 16)
                steps.append(s_load)

                def mk_red(j3):
                    def f():
                        if j3 == 0:
                            wl.wait(ve)
                        cw = min(1024, I - j3 * 1024)
                        wq.inc(ve.tensor_reduce(
                            mvt[:, j3:j3 + 1],
                            v_wd3[:, j3 * 1024:j3 * 1024 + cw],
                            AX.X, ALU.max, apply_absolute_value=True), 1)
                    return f
                for j3 in range(-(-I // 1024)):
                    steps.append(mk_red(j3))

                def s_scale():
                    wsv.wait(ve)
                    wq.wait(ve)
                    nred = -(-I // 1024)
                    wq.inc(ve.tensor_reduce(mvt[:, 3:4], mvt[:, 0:nred], AX.X,
                                            ALU.max), 1)
                    wq.wait(sc)
                    wsv.inc(sc.mul(swd[:, mh:mh + 1], mvt[:, 3:4],
                                   1.0 / 127.0), 1)
                    wsv.wait(ve)
                    wq.inc(ve.reciprocal(mvt[:, 4:5], swd[:, mh:mh + 1]), 1)
                steps.append(s_scale)

                def mk_piece(j):
                    def f():
                        cw = min(QW, I - j * QW)
                        io = (j % 2) * 512
                        if j == 0:
                            r_wdq.acq(gp, 0)
                            wq.wait(ve)
                        r_wi32.acq(ve, j % 2)
                        wq.inc(ve.tensor_scalar(sc_i32[:, io:io + cw],
                                                v_wd3[:, j * QW:j * QW + cw],
                                                mvt[:, 4:5], None, ALU.mult), 1)
                        if j == NQ - 1:
                            r_wdl.rel(0, wq)
                        wq.wait(gp)
                        wqg.inc(gp.tensor_scalar(b_wdq[:, j * QW:j * QW + cw],
                                                 sc_i32[:, io:io + cw],
                                                 127.0, -128.0,
                                                 ALU.min, ALU.max), 1)
                        r_wi32.rel(j % 2, wqg)
                    return f
                for j in range(NQ):
                    steps.append(mk_piece(j))

                def s_fin():
                    wqg.wait(te)
                    r_wdpt.acq(te, 0)
                    for k in range(IP):
                        inst = te.transpose(pt[:, k * 128:(k + 1) * 128],
                                            b_wdq[:, k * 128:(k + 1) * 128],
                                            id_bf[:, :])
                    t1.inc(inst, 1)
                    r_wdq.rel(0, t1)
                    t1.wait(sc)
                    r_wdev.acq(sc, 0)
                    inst = sc.activation(b_qev[:, 0:IP * 128], pt[:, 0:IP * 128],
                                         AF.Copy)
                    ea.inc(inst, 1)
                    r_wdpt.rel(0, ea)
                    ea.wait(sy)
                    ws.inc(sy.dma_start(
                        AP(wdT_bh, mh * 128 * IP * 128,
                           [[IP * 128, 128], [1, IP * 128]]),
                        b_qev[:, 0:IP * 128]), 16)
                    r_wdev.rel(0, ws)
                steps.append(s_fin)
                return steps

            wd_q = []
            wd_i = [0]

            def wd_tick():
                if not wd_q and wd_i[0] < MP:
                    wd_q.extend(wd_steps(wd_i[0]))
                    wd_i[0] += 1
                if wd_q:
                    wd_q.pop(0)()

            for r in range(IP):
                wbg = (b_g2a, b_g2b)[r % 2]
                wbu = (b_u2a, b_u2b)[r % 2]
                r_wg2.acq(sy, r)
                r_wu2.acq(sy, r)
                if r == 0:
                    sm["cg"].wait(sy, 3)
                was = sm[("wa0", "wa1")[r % 2]]
                was.inc(sy.dma_start(
                    wbg[:, 0:MP * 128],
                    AP(wgT_bh, r * 128 * MP * 128,
                       [[MP * 128, 128], [1, MP * 128]])), 16)
                was.inc(sy.dma_start(
                    wbu[:, 0:MP * 128],
                    AP(wuT_bh, r * 128 * MP * 128,
                       [[MP * 128, 128], [1, MP * 128]])), 16)
                was.wait(te)
                if r == 0:
                    q2.wait(te)
                    wqg.wait(te)
                for ch in range(NCH):
                    idx = r * NCH + ch
                    bg, bu = (pb[0], pb[2]) if idx % 2 == 0 else (pb[1], pb[3])
                    r_bkg.acq(te, idx)
                    for k in range(MP):
                        inst = te.matmul(
                            bg[:, 0:TC], wbg[:, k * 128:(k + 1) * 128],
                            xint[:, k * L + ch * TC:k * L + (ch + 1) * TC],
                            start=(k == 0), stop=(k == MP - 1))
                    m2.inc(inst, 1)
                    g_t = m2.n
                    r_bku.acq(te, idx)
                    for k in range(MP):
                        inst = te.matmul(
                            bu[:, 0:TC], wbu[:, k * 128:(k + 1) * 128],
                            xint[:, k * L + ch * TC:k * L + (ch + 1) * TC],
                            start=(k == 0), stop=(k == MP - 1))
                    m2.inc(inst, 1)
                    if ch == NCH - 1:
                        r_wg2.rel(r, m2)
                        r_wu2.rel(r, m2)
                    m2.wait(sc, g_t)
                    gbuf = (v_gsa, v_gsb)[idx % 2]
                    r_gs.acq(sc, idx)
                    gc = (r // TPC) * 2 * TPC + (r % TPC)
                    inst = sc.activation(gbuf[:, 0:TC], bg[:, 0:TC],
                                         AF.Silu, scale=esgu[:, gc:gc + 1])
                    e2.inc(inst, 1)
                    r_bkg.rel(idx, e2)
                    e2.wait(ve)
                    m2.wait(ve)
                    hbuf = (v_hsa, v_hsb)[idx % 2]
                    r_hb.acq(ve, idx)
                    inst = ve.scalar_tensor_tensor(
                        hbuf[:, 0:TC], bu[:, 0:TC],
                        esgu[:, gc + TPC:gc + TPC + 1], gbuf[:, 0:TC],
                        ALU.mult, ALU.mult)
                    e2v.inc(inst, 1)
                    r_bku.rel(idx, e2v)
                    r_gs.rel(idx, e2v)
                    e2v.wait(ve)
                    inst = ve.tensor_reduce(hmax[:, idx:idx + 1], hbuf[:, 0:TC],
                                            AX.X, ALU.max,
                                            apply_absolute_value=True)
                    g2.inc(inst, 1)
                    e2v.wait(sc)
                    h2.inc(sc.dma_start(h_h[r * 128:(r + 1) * 128,
                                            ch * TC:(ch + 1) * TC],
                                        hbuf[:, 0:TC]), 16)
                    r_hb.rel(idx, h2)
                    if idx % 3 != 2:
                        wd_tick()
            while wd_q or wd_i[0] < MP:
                wd_tick()
            g2.wait(ve)
            inst = ve.tensor_reduce(hmax[:, IP * NCH:IP * NCH + 1],
                                    hmax[:, 0:IP * NCH], AX.X, ALU.max)
            q2.inc(inst, 1)
            q2.wait(sy)
            n1.inc(sy.dma_start(arh_src[:, :], hmax[:, IP * NCH:IP * NCH + 1]), 16)
            n1.wait(gp)
            cc.inc(gp.collective_compute(
                "AllReduce", ALU.max, replica_groups=rg,
                ins=[arh_src.ap().opt()], outs=[arh_dst.ap().opt()]), 1)

            barrier()

            m3, e3, q3, o3 = sm["m3"], sm["e3"], sm["q3"], sm["o3"]
            es3, ev3 = sm["es3"], sm["ev3"]
            l2 = sm["l2"]
            cc.wait(gp, 3)
            l2.inc(gp.dma_start(sclh[:, 0:1], arh_dst[:, :]), 16)
            pm = sm["pm"]
            l2.wait(te)
            inst = te.transpose(pb[3][0:1, 0:128], sclh[:, 0:1], id_f32[:, :])
            pm.inc(inst, 1)
            pm.wait(ve)
            inst = ve.reduce_max(sclh[0:1, 0:1], pb[3][0:1, 0:128], AX.X)
            pm.inc(inst, 1)
            pm.wait(te)
            inst = te.matmul(pb[2][:, 0:1], c_ones[0:1, 0:128], sclh[0:1, 0:1],
                             start=True, stop=True)
            pm.inc(inst, 1)
            pm.wait(ve)
            inst = ve.tensor_copy(sclh[:, 1:2], pb[2][:, 0:1])
            pm.inc(inst, 1)
            pm.wait(ve)
            inst = ve.scalar_tensor_tensor(sclh[:, 2:3], c_sig[:, 1:2],
                                           1.0 / 127.0, sclh[:, 1:2],
                                           ALU.mult, ALU.mult)
            pm.inc(inst, 1)
            pm.wait(ve)
            inst = ve.reciprocal(sclh[:, 3:4], sclh[:, 2:3])
            pm.inc(inst, 1)
            pm.wait(ve)
            sm["wsv"].wait(ve)
            inst = ve.tensor_scalar_mul(esd[:, 0:MP], swd[:, 0:MP], sclh[:, 2:3])
            q3.inc(inst, 1)

            r_h3 = Res(4)
            r_hq = Res(3)
            r_i32 = Res(4)
            r_bank = Res(4)
            r_g3 = Res(1)
            r_t3 = Res(1)
            r_pt3 = Res(2)
            r_x3 = Res(2)
            r_o3 = Res(1, dma=True)
            qg = sm["g2"]
            h2.wait(sy)
            ws.wait(sy)
            wdl = sm["wa0"]
            wdl.inc(sy.dma_start(
                b_wdr[:, 0:MP * IP * 128],
                AP(wdT_bh, 0, [[IP * 128, 128], [128 * IP * 128, MP],
                               [1, IP * 128]])), 16)
            ldb = [0]
            q3_q = []
            qg_marks = {}

            def quant_steps(ch):
                hqv = b_hqs[ch % 3]
                g0s = list(range(0, IP, 2))

                def mk_load(gi):
                    g0 = g0s[gi]
                    sl = ldb[0] % 4
                    ldb[0] += 1

                    def f():
                        gn = min(2, IP - g0)
                        if g0 == 0:
                            r_hq.acq(sy, ch)
                        r_h3.acq(sy, sl)
                        da = sm[("da0", "da1", "da2", "da3")[sl]]
                        da.inc(sy.dma_start(
                            v_h3s[sl][:, 0:gn * TC3],
                            AP(h_h, g0 * 128 * L + ch * TC3,
                               [[L, 128], [128 * L, gn], [1, TC3]])), 16)
                    return f, sl

                def mk_piece(gi, sl):
                    g0 = g0s[gi]

                    def f():
                        gn = min(2, IP - g0)
                        buf = v_h3s[sl]
                        da = sm[("da0", "da1", "da2", "da3")[sl]]
                        da.wait(ve)
                        for i in range(gn):
                            r = g0 + i
                            r_i32.acq(ve, r % 4)
                            q3.inc(ve.tensor_scalar(
                                sc_i32[:, (r % 4) * TC3:(r % 4 + 1) * TC3],
                                buf[:, i * TC3:(i + 1) * TC3],
                                sclh[:, 3:4], None, ALU.mult), 1)
                        r_h3.rel(sl, q3)
                        q3.wait(gp)
                        qg.inc(gp.tensor_scalar(
                            hqv[:, g0 * TC3:(g0 + gn) * TC3],
                            sc_i32[:, (g0 % 4) * TC3:(g0 % 4 + gn) * TC3],
                            127.0, -128.0, ALU.min, ALU.max), 1)
                        for i2 in range(gn):
                            r_i32.rel(g0 + i2, qg)
                        if gi == len(g0s) - 1:
                            qg_marks[ch] = qg.n
                    return f

                loads = [mk_load(gi) for gi in range(len(g0s))]
                steps = []
                depth = min(2, len(g0s))
                for gi in range(depth):
                    steps.append(loads[gi][0])
                for gi in range(len(g0s)):
                    if gi + depth < len(g0s):
                        steps.append(loads[gi + depth][0])
                    steps.append(mk_piece(gi, loads[gi][1]))
                return steps

            def q3_tick():
                if q3_q:
                    q3_q.pop(0)()

            def mm_chunk(ch):
                hqv = b_hqs[ch % 3]
                poff = (ch % 2) * 2048
                for mh in range(MP):
                    if ch == 0 and mh == 0:
                        wdl.wait(te)
                    if mh == 0:
                        qg.wait(te, qg_marks[ch])
                    bank = pb[mh % 4]
                    r_bank.acq(te, mh % 4)
                    for r in range(IP):
                        inst = te.matmul(bank[:, 0:TC3],
                                         b_wdr[:, (mh * IP + r) * 128:
                                               (mh * IP + r + 1) * 128],
                                         hqv[:, r * TC3:(r + 1) * TC3],
                                         start=(r == 0), stop=(r == IP - 1))
                    m3.inc(inst, 1)
                    if mh == MP - 1:
                        r_hq.rel(ch, m3)
                    m3.wait(sc)
                    r_g3.acq(sc, 0)
                    inst = sc.activation(v_g3[:, 0:TC3], bank[:, 0:TC3], AF.Silu,
                                         scale=esd[:, mh:mh + 1])
                    es3.inc(inst, 1)
                    r_bank.rel(mh % 4, es3)
                    es3.wait(ve)
                    r_t3.acq(ve, 0)
                    ev3.inc(ve.tensor_tensor(
                        b_t3[:, mh * TC3:(mh + 1) * TC3],
                        zs[:, mh * L + ch * TC3:mh * L + (ch + 1) * TC3],
                        v_g3[:, 0:TC3], ALU.mult), 1)
                    r_g3.rel(0, ev3)
                    for _ in range(3):
                        q3_tick()
                ev3.wait(te)
                r_pt3.acq(te, ch)
                for mh in range(MP):
                    for tt in range(CT3):
                        inst = te.transpose(
                            pt[:, poff + (tt * MP + mh) * 128:
                               poff + (tt * MP + mh + 1) * 128],
                            b_t3[:, mh * TC3 + tt * 128:
                                 mh * TC3 + (tt + 1) * 128], id_bf[:, :])
                m3.inc(inst, 1)
                r_t3.rel(0, m3)
                return m3.n

            def fin_chunk(ch, m3_t):
                poff = (ch % 2) * 2048
                for tt in range(CT3):
                    t_glob = ch * CT3 + tt
                    xbuf = (v_x3a, v_x3b)[t_glob % 2]
                    r_x3.acq(sy, t_glob)
                    dbm = sm[("db0", "db1")[t_glob % 2]]
                    dbm.inc(sy.dma_start(xbuf[:, 0:M],
                                         xb[t_glob * 128:(t_glob + 1) * 128, :]), 16)
                    dbm.wait(ve)
                    m3.wait(ve, m3_t)
                    ov = (v_o3, v_o3b)[t_glob % 2]
                    osem = sm[("o3a", "o3b")[t_glob % 2]]
                    osem.wait(ve)
                    ev3.inc(ve.tensor_tensor(
                        ov[:, 0:M], xbuf[:, 0:M],
                        pt[:, poff + tt * MP * 128:poff + (tt + 1) * MP * 128],
                        ALU.add), 1)
                    r_x3.rel(t_glob, ev3)
                    if tt == CT3 - 1:
                        r_pt3.rel(ch, ev3)
                    ev3.wait(sy)
                    osem.inc(
                        sy.dma_start(out[t_glob * 128:(t_glob + 1) * 128, :],
                                     ov[:, 0:M]), 16)

            ve.memset(v_o3[:, 0:M], 0.0)
            ve.memset(v_o3b[:, 0:M], 0.0)
            q3.inc(ve.drain(), 1)
            q3_q.extend(quant_steps(0))
            while q3_q:
                q3_tick()
            if NC3 > 1:
                q3_q.extend(quant_steps(1))
            fin_pend = []
            for ch in range(NC3):
                if ch + 2 < NC3:
                    q3_q.extend(quant_steps(ch + 2))
                m3t = mm_chunk(ch)
                if fin_pend:
                    fin_chunk(*fin_pend.pop(0))
                fin_pend.append((ch, m3t))
                while ch + 1 >= NC3 and q3_q:
                    q3_tick()
            while fin_pend:
                fin_chunk(*fin_pend.pop(0))

            sm["o3a"].wait(real)
            sm["o3b"].wait(real)
            barrier()

        @block.sync
        def _(e):
            emit("sync", e)

        @block.vector
        def _(e):
            emit("vector", e)

        @block.scalar
        def _(e):
            emit("scalar", e)

        @block.gpsimd
        def _(e):
            emit("gpsimd", e)

        @block.tensor
        def _(e):
            emit("tensor", e)

    return nc


def _in_maps(cfg, x, W_in, W_xp, W_out, Dmat, wg_, wu_, wd_, b_in_, b_xp_,
             b_out_, gamma_, beta_, A_log_, clip_x_, clip_d_):
    L, M, S = cfg["L"], cfg["M"], cfg["S"]
    B = N_CORES
    W = L // B
    ITOT = wg_.shape[0]
    TPC = -(-(ITOT // 128) // B)

    def _shard(w, c):
        lo, hi = c * TPC * 128, min((c + 1) * TPC * 128, ITOT)
        pad = np.full((TPC * 128, w.shape[1]), 1.0, np.float32)
        pad[0:hi - lo] = w[lo:hi]
        return np.ascontiguousarray(pad)

    shared = dict(
        winT=np.ascontiguousarray(W_in.T), wxpT=np.ascontiguousarray(W_xp.T),
        woutT=np.ascontiguousarray(W_out.T), dmat=np.ascontiguousarray(Dmat),
        wd=np.ascontiguousarray(wd_),
        b_in=np.ascontiguousarray(b_in_), b_xp=np.ascontiguousarray(b_xp_),
        b_out=np.ascontiguousarray(b_out_),
        gamma=np.ascontiguousarray(gamma_), beta=np.ascontiguousarray(beta_),
        alog=np.ascontiguousarray(np.asarray(A_log_).reshape(S)),
        clipx=np.ascontiguousarray(np.asarray(clip_x_).reshape(1)),
        clipd=np.ascontiguousarray(np.asarray(clip_d_).reshape(1)),
    )
    maps = []
    for c in range(B):
        ev = np.full(B, 1e9, np.float32)
        for k in range(c):
            ev[k] = float(W * (c - 1 - k))
        m = dict(shared)
        m.update(
            wg_my=_shard(wg_, c), wu_my=_shard(wu_, c),
            xb=np.ascontiguousarray(x[c]),
            xt=np.ascontiguousarray(
                x[:, c * W:(c + 1) * W, :].transpose(1, 0, 2).reshape(L, M)),
            evec=ev,
        )
        maps.append(m)
    return maps


def kernel(L0_attn_x, L0_attn_A_log, L0_attn_Dmat, L0_attn_W_in, L0_attn_b_in,
           L0_attn_W_xp, L0_attn_b_xp, L0_attn_W_out, L0_attn_b_out,
           L0_attn_gamma, L0_attn_beta, L0_ffn_clip_x, L0_ffn_clip_down,
           L0_ffn_w_gate, L0_ffn_w_up, L0_ffn_w_down):
    cfg = CFG
    f32 = lambda a: np.asarray(a, dtype=np.float32)
    maps = _in_maps(cfg, f32(L0_attn_x), f32(L0_attn_W_in), f32(L0_attn_W_xp),
                    f32(L0_attn_W_out), f32(L0_attn_Dmat), f32(L0_ffn_w_gate),
                    f32(L0_ffn_w_up), f32(L0_ffn_w_down), f32(L0_attn_b_in),
                    f32(L0_attn_b_xp), f32(L0_attn_b_out), f32(L0_attn_gamma),
                    f32(L0_attn_beta), f32(L0_attn_A_log), f32(L0_ffn_clip_x),
                    f32(L0_ffn_clip_down))
    nc = build_nc(cfg)
    res = run_bass_kernel_spmd(nc, maps, core_ids=list(range(N_CORES)))
    return np.stack([res.results[c]["out"] for c in range(N_CORES)], axis=0)


# version 27
# speedup vs baseline: 1.0161x; 1.0161x over previous
import contextlib
import numpy as np
import concourse.bass as bass
import concourse.mybir as mybir
from concourse.bass_utils import run_bass_kernel_spmd

F32 = mybir.dt.float32
BF16 = mybir.dt.bfloat16
I32 = mybir.dt.int32
AF = mybir.ActivationFunctionType
ALU = mybir.AluOpType
AX = mybir.AxisListType
AP = bass.AP

CFG = dict(L=4096, M=1024, S=64, I=2816, TC=512, TC3=256)
N_CORES = 8


class _FakeInst:
    def then_inc(self, sem, n=1):
        return self


_FAKE_INST = _FakeInst()


class _FakeEngine:
    def __getattr__(self, name):
        return lambda *a, **k: _FAKE_INST


class Sem:

    def __init__(self, handle):
        self.h = handle
        self.n = 0

    def inc(self, inst, k=1):
        inst.then_inc(self.h, k)
        self.n += k

    def wait(self, eng, val=None):
        eng.wait_ge(self.h, self.n if val is None else val)


class Res:

    def __init__(self, nslots, dma=False):
        self.k = nslots
        self.dma = dma
        self.p = [None] * nslots

    def acq(self, eng, idx):
        g = self.p[idx % self.k]
        if g is not None:
            g[0].wait(eng, None if self.dma else g[1])

    def rel(self, idx, sem):
        self.p[idx % self.k] = (sem, sem.n)


class View:

    def __init__(self, t, off, cols):
        self.t, self.off, self.cols = t, off, cols

    def __getitem__(self, key):
        p, c = key
        start = self.off + (c.start or 0)
        stop = self.off + (c.stop if c.stop is not None else self.cols)
        if c.step in (None, 1):
            return self.t[p, start:stop]
        return self.t[p, start:stop:c.step]


def build_nc(cfg):
    L, M, S, I = cfg["L"], cfg["M"], cfg["S"], cfg["I"]
    TC, TC3 = cfg["TC"], cfg["TC3"]
    B = N_CORES
    MP, NP, IP = M // 128, 2 * M // 128, I // 128
    LT = L // 128
    NCH = L // TC
    NC3 = L // TC3
    CT = TC // 128
    CT3 = TC3 // 128
    W = L // B
    GSZ = min(M, 512)
    GN = -(-M // GSZ)
    TL = 128 * 128
    QP = 512
    AFC = max(MP * S + M, 2048 + 4 * M + 2 * TC, 4 * TC + I, L,
              3 * 1024, 4 * TC, 4 * M + 9 * TC3)
    ABC = max(2 * I + 5 * MP * 128, 4 * MP * 128 + 2 * I,
              max(2 * I + 2 * MP * 128, 4 * MP * 128 + I) + I,
              M + 2 * MP * TC, MP * TC + 2 * MP * 128 + L, 2 * IP * TC3)

    nc = bass.Bass("TRN2", target_bir_lowering=False)

    xb = nc.dram_tensor("xb", [L, M], F32, kind="ExternalInput")
    xt = nc.dram_tensor("xt", [L, M], F32, kind="ExternalInput")
    winT = nc.dram_tensor("winT", [M, 2 * M], F32, kind="ExternalInput")
    wxpT = nc.dram_tensor("wxpT", [M, S], F32, kind="ExternalInput")
    woutT = nc.dram_tensor("woutT", [M, M], F32, kind="ExternalInput")
    dmat = nc.dram_tensor("dmat", [S, M], F32, kind="ExternalInput")
    TPC = -(-IP // B)
    IPAD = TPC * B
    wg_my = nc.dram_tensor("wg_my", [TPC * 128, M], F32, kind="ExternalInput")
    wu_my = nc.dram_tensor("wu_my", [TPC * 128, M], F32, kind="ExternalInput")
    wd = nc.dram_tensor("wd", [M, I], F32, kind="ExternalInput")
    b_in = nc.dram_tensor("b_in", [2 * M], F32, kind="ExternalInput")
    b_xp = nc.dram_tensor("b_xp", [S], F32, kind="ExternalInput")
    b_out = nc.dram_tensor("b_out", [M], F32, kind="ExternalInput")
    gamma = nc.dram_tensor("gamma", [M], F32, kind="ExternalInput")
    beta = nc.dram_tensor("beta", [M], F32, kind="ExternalInput")
    alog = nc.dram_tensor("alog", [S], F32, kind="ExternalInput")
    clipx = nc.dram_tensor("clipx", [1], F32, kind="ExternalInput")
    clipd = nc.dram_tensor("clipd", [1], F32, kind="ExternalInput")
    evec = nc.dram_tensor("evec", [B], F32, kind="ExternalInput")
    out = nc.dram_tensor("out", [L, M], F32, kind="ExternalOutput")

    xnfm_h = nc.dram_tensor("xnfm_h", [MP * LT * TL], F32)
    h_h = nc.dram_tensor("h_h", [IP * 128, L], F32)
    winT_bh = nc.dram_tensor("winT_bh", [NP * MP * TL], BF16)
    woutT_bh = nc.dram_tensor("woutT_bh", [MP * MP * TL], BF16)
    wgT_bh = nc.dram_tensor("wgT_bh", [IPAD * MP * TL], BF16)
    wuT_bh = nc.dram_tensor("wuT_bh", [IPAD * MP * TL], BF16)
    wgT_src = nc.dram_tensor("wgT_src", [TPC * MP * TL], BF16)
    wuT_src = nc.dram_tensor("wuT_src", [TPC * MP * TL], BF16)
    sws_src = nc.dram_tensor("sws_src", [128, 2 * TPC], F32)
    sws_dst = nc.dram_tensor("sws_dst", [128 * B, 2 * TPC], F32)
    wdT_bh = nc.dram_tensor("wdT_bh", [MP * IP * TL], BF16)
    arx_src = nc.dram_tensor("arx_src", [128, 1], F32)
    arx_dst = nc.dram_tensor("arx_dst", [128, 1], F32)
    arh_src = nc.dram_tensor("arh_src", [128, 1], F32)
    arh_dst = nc.dram_tensor("arh_dst", [128, 1], F32)
    ag_src = nc.dram_tensor("ag_src", [S, B], F32)
    ag_dst = nc.dram_tensor("ag_dst", [B * S, B], F32)

    rg = [list(range(B))]

    sem_names = [
        "cl", "c0", "wl", "ws", "x1", "v1", "s1", "t1", "e1", "n1", "cc",
        "xa", "va", "sa", "ta", "ea", "wa", "ma", "pa", "sc", "ld", "l2",
        "mc", "ec", "q2", "m2", "e2", "g2", "h2", "m3", "e3", "q3", "o3",
        "bar", "da0", "da1", "db0", "db1", "wa0", "wa1", "pm", "o3a", "o3b",
        "es3", "ev3", "e2v", "wq", "wsv", "da2", "da3", "wqs", "wqg",
        "cg", "wsg", "nsg",
    ]

    with contextlib.ExitStack() as ST:
        def sb(name, shape, dt):
            return ST.enter_context(nc.sbuf_tensor(name, shape, dt))

        def psum(name, shape, dt):
            return ST.enter_context(nc.psum_tensor(name, shape, dt))

        zs = sb("zs", [128, MP * L], BF16)
        xint = sb("xint", [128, max(MP * L, NP * MP * 128,
                                    MP * IP * 128 + MP * TC3 + IP * TC3)], BF16)
        ubuf = sb("ubuf", [S, L], F32)
        af = sb("af", [128, AFC], F32)
        ab = sb("ab", [128, ABC], BF16)
        sc_i32 = sb("sc_i32", [128, 1024], I32)
        c_gam = sb("c_gam", [128, MP], F32)
        c_bet = sb("c_bet", [128, MP], F32)
        c_bin = sb("c_bin", [128, NP], F32)
        c_bout = sb("c_bout", [128, MP], F32)
        c_bxp = sb("c_bxp", [S, 1], F32)
        c_alog = sb("c_alog", [S, 1], F32)
        c_alog_r = sb("c_alog_r", [1, S], F32)
        c_ae = sb("c_ae", [S, 1], F32)
        c_a = sb("c_a", [S, 1], F32)
        c_arep = sb("c_arep", [S, W], F32)
        c_evec = sb("c_evec", [1, B], F32)
        c_coef = sb("c_coef", [S, B], F32)
        c_aer = sb("c_aer", [1, S], F32)
        c_clip = sb("c_clip", [1, 2], F32)
        c_cst = sb("c_cst", [128, 2], F32)
        c_ones = sb("c_ones", [1, 128], F32)
        c_sig = sb("c_sig", [128, 2], F32)
        id_bf = sb("id_bf", [128, 128], BF16)
        id_f32 = sb("id_f32", [128, 128], F32)
        iota_t = sb("iota_t", [128, 128], F32)
        pid_t = sb("pid_t", [128, 1], F32)
        wxp_sb = sb("wxp_sb", [128, MP * S], BF16)
        dm_sb = sb("dm_sb", [S, M], BF16)
        swl = sb("swl", [128, 2 * TPC], F32)
        sws = sb("sws", [128, 16 * TPC], F32)
        esgu = sb("esgu", [128, 16 * TPC], F32)
        swd = sb("swd", [128, MP], F32)
        esd = sb("esd", [128, MP], F32)
        amax = sb("amax", [128, LT + 2], F32)
        hmax = sb("hmax", [128, IP * NCH + 2], F32)
        sclx = sb("sclx", [128, 4], F32)
        sclh = sb("sclh", [128, 4], F32)
        hloc = sb("hloc", [S, B], F32)
        agb = sb("agb", [S, B * B], F32)
        carry = sb("carry", [S, 2 * B], F32)
        mvt = sb("mvt", [128, 16], F32)
        stats = sb("stats", [128, 12 * GN], F32)
        pb = [psum(f"pb{i}", [128, 512], F32) for i in range(4)]
        pt = psum("pt", [128, 4096], BF16)

        semh = {n: ST.enter_context(nc.semaphore("m_" + n)) for n in sem_names}
        block = ST.enter_context(nc.Block())

        v_xa = View(af, 0, M)
        v_xb = View(af, M, M)
        v_xn = View(af, 2 * M, M)
        v_ln2 = View(af, 3 * M, M)
        v_fma = View(af, 3 * M, M)
        v_fmb = View(af, 4 * M, M)
        v_p1s = [View(af, 2048 + 4 * M, TC),
                 View(af, 2048 + 4 * M + TC, TC)]
        v_qga = View(af, 0, 1024)
        v_qgb = View(af, 1024, 1024)
        _qbw = min(1024, L)
        v_qxc = View(af, 2048 + 2 * TC, _qbw)
        v_qxd = View(af, 2048 + 2 * TC + _qbw, _qbw)
        v_fxa = View(af, 2048, M)
        v_fxb = View(af, 2048 + M, M)
        v_fxn = View(af, 2048 + 2 * M, M)
        v_fst = View(af, 2048 + 3 * M, M)
        v_fst2 = View(af, 2048 + 4 * M, M)
        v_x4 = View(af, 2048 + 2 * M, M)
        v_x5 = View(af, 2048 + 3 * M, M)
        v_wd3 = View(af, 4 * TC, I)
        v_xfa = View(af, 2048, TC)
        v_xfb = View(af, 2048 + TC, TC)
        v_hsa = View(af, 0, TC)
        v_hsb = View(af, TC, TC)
        v_gsa = View(af, 2 * TC, TC)
        v_gsb = View(af, 3 * TC, TC)
        v_x3a = View(af, 0, M)
        v_x3b = View(af, M, M)
        v_o3 = View(af, 2 * M, M)
        v_g3 = View(af, 3 * M, TC3)
        v_h3s = [View(af, 3 * M + TC3 + j * 2 * TC3, 2 * TC3)
                 for j in range(4)]
        v_o3b = View(af, 3 * M + 9 * TC3, M)
        v_wxps = View(af, 0, MP * S)
        v_dms = View(af, MP * S, M)
        v_qwa = View(af, 0, I)
        v_qwd = View(af, I, QP)
        v_qgc2 = View(af, 2048, QP)
        b_slA = View(ab, 0, MP * TC)
        b_slB = View(ab, MP * TC, MP * TC)
        b_preA = View(ab, 2 * MP * TC, M)
        b_preB = View(ab, 2 * MP * TC + M, M)
        b_ysz = View(ab, 0, MP * TC)
        b_cwa = View(ab, MP * TC, MP * 128)
        b_cwb = View(ab, MP * TC + MP * 128, MP * 128)
        b_hbf = View(ab, MP * TC + 2 * MP * 128, L)
        b_qsa = View(ab, 0, I)
        b_qsb = View(ab, I, I)
        b_pca = View(ab, 2 * I, MP * 128)
        b_pcb = View(ab, 2 * I + MP * 128, MP * 128)
        b_qev0 = View(ab, 2 * I + 2 * MP * 128, MP * 128)
        b_qev1 = View(ab, 2 * I + 3 * MP * 128, MP * 128)
        b_pcc = View(ab, 2 * I + 4 * MP * 128, MP * 128)
        _wdq_off = 4 * MP * 128
        _qev_off = max(2 * I + 2 * MP * 128, _wdq_off + I)
        b_wdq = View(ab, _wdq_off, I)
        b_qev = View(ab, _qev_off, I)
        b_g2a = View(ab, 0, MP * 128)
        b_g2b = View(ab, MP * 128, MP * 128)
        b_u2a = View(ab, 2 * MP * 128, MP * 128)
        b_u2b = View(ab, 3 * MP * 128, MP * 128)
        b_wdr = View(xint, 0, MP * IP * 128)
        b_t3 = View(xint, MP * IP * 128, MP * TC3)
        b_hqs = [View(ab, 0, IP * TC3), View(ab, IP * TC3, IP * TC3),
                 View(xint, MP * IP * 128 + MP * TC3, IP * TC3)]

        def emit(which, eng):
            sy = eng if which == "sync" else _FakeEngine()
            ve = eng if which == "vector" else _FakeEngine()
            sc = eng if which == "scalar" else _FakeEngine()
            gp = eng if which == "gpsimd" else _FakeEngine()
            te = eng if which == "tensor" else _FakeEngine()
            real = eng

            sm = {n: Sem(h) for n, h in semh.items()}
            dma_sems_sync = [sm[n] for n in
                             ("cl", "wl", "ws", "x1", "n1", "xa", "wa", "ld",
                              "l2", "o3", "o3a", "o3b", "da0", "da1",
                              "da2", "da3", "db0", "db1", "wa0", "wa1",
                              "wsg", "nsg")]
            bar_target = [0]

            def barrier():
                if which == "sync":
                    for s_ in dma_sems_sync:
                        s_.wait(real)
                if which == "scalar":
                    sm["h2"].wait(real)
                real.drain().then_inc(sm["bar"].h, 1)
                bar_target[0] += 5
                real.wait_ge(sm["bar"].h, bar_target[0])

            cl, c0 = sm["cl"], sm["c0"]
            cl.inc(sy.dma_start(c_gam[:, :], AP(gamma, 0, [[1, 128], [128, MP]]),
                     allow_slow_non_contiguous=True), 16)
            cl.inc(sy.dma_start(c_bet[:, :], AP(beta, 0, [[1, 128], [128, MP]]),
                     allow_slow_non_contiguous=True), 16)
            cl.inc(sy.dma_start(c_bin[:, :], AP(b_in, 0, [[1, 128], [128, NP]]),
                     allow_slow_non_contiguous=True), 16)
            cl.inc(sy.dma_start(c_bout[:, :], AP(b_out, 0, [[1, 128], [128, MP]]),
                     allow_slow_non_contiguous=True), 16)
            cl.inc(sy.dma_start(c_bxp[:, :], AP(b_xp, 0, [[1, S], [1, 1]])), 16)
            cl.inc(sy.dma_start(c_alog[:, :], AP(alog, 0, [[1, S], [1, 1]])), 16)
            cl.inc(sy.dma_start(c_alog_r[:, :], AP(alog, 0, [[S, 1], [1, S]])), 16)
            cl.inc(sy.dma_start(c_evec[:, :], AP(evec, 0, [[B, 1], [1, B]])), 16)
            cl.inc(sy.dma_start(c_clip[:, 0:1], AP(clipx, 0, [[1, 1], [1, 1]])), 16)
            cl.inc(sy.dma_start(c_clip[:, 1:2], AP(clipd, 0, [[1, 1], [1, 1]])), 16)
            cl.inc(sy.dma_start(
                v_wxps[:, 0:MP * S],
                AP(wxpT, 0, [[S, 128], [128 * S, MP], [1, S]])), 16)
            cl.inc(sy.dma_start(v_dms[0:S, 0:M], dmat[:, :]), 16)

            gp.iota(iota_t[:, :], [[1, 128]], channel_multiplier=0,
                    allow_small_or_imprecise_dtypes=True)
            gp.iota(pid_t[:, :], [[0, 1]], channel_multiplier=1,
                    allow_small_or_imprecise_dtypes=True)
            c0.inc(gp.drain(), 1)
            c0.wait(ve, 1)
            ve.tensor_scalar(id_bf[:, :], iota_t[:, :], pid_t[:, 0:1], None,
                             ALU.is_equal)
            ve.tensor_scalar(id_f32[:, :], iota_t[:, :], pid_t[:, 0:1], None,
                             ALU.is_equal)
            cl.wait(ve)
            ve.tensor_copy(wxp_sb[:, :], v_wxps[:, 0:MP * S])
            ve.tensor_copy(dm_sb[:, :], v_dms[0:S, 0:M])
            ve.memset(c_cst[:, 0:1], 1e-5)
            ve.memset(c_cst[:, 1:2], -1.0)
            ve.memset(c_ones[0:1, :], 1.0)
            c0.inc(ve.drain(), 1)

            cl.wait(sc)
            c0.wait(sc, 2)
            sc.activation(c_sig[0:1, 0:2], c_clip[0:1, 0:2], AF.Sigmoid)
            sc.activation(c_ae[:, 0:1], c_alog[:, 0:1], AF.Exp)
            sc.activation(c_aer[0:1, :], c_alog_r[0:1, :], AF.Exp)
            sc.drain()
            sc.activation(c_a[:, 0:1], c_ae[:, 0:1], AF.Exp, scale=c_cst[0:S, 1:2])
            c0.inc(sc.drain(), 1)
            c0.wait(ve, 3)
            ve.memset(c_arep[:, :], 1.0)
            ve.drain()
            ve.tensor_scalar_mul(c_arep[:, :], c_arep[:, :], c_a[:, 0:1])
            c0.inc(ve.drain(), 1)
            c0.wait(te, 3)
            te.matmul(pb[2][:, 0:2], c_ones[0:1, 0:128], c_sig[0:1, 0:2],
                      start=True, stop=True)
            c0.inc(te.drain(), 1)
            c0.wait(ve, 5)
            ve.tensor_copy(c_sig[:, 0:2], pb[2][:, 0:2])
            c0.inc(ve.drain(), 1)
            c0.wait(te, 3)
            te.matmul(pb[3][0:S, 0:B], c_aer[0:1, 0:S], c_evec[0:1, 0:B],
                      start=True, stop=True)
            c0.inc(te.drain(), 1)
            c0.wait(sc, c0.n)
            sc.activation(c_coef[:, :], pb[3][0:S, 0:B], AF.Exp, scale=c_cst[0:S, 1:2])
            c0.inc(sc.drain(), 1)

            wl, ws, e1 = sm["wl"], sm["ws"], sm["e1"]
            t1, ea = sm["t1"], sm["ea"]
            x1, v1, s1, n1 = sm["x1"], sm["v1"], sm["s1"], sm["n1"]
            wq, wsv = sm["wq"], sm["wsv"]
            wqs, wqg = sm["wqs"], sm["wqg"]
            cg, wsg, nsg = sm["cg"], sm["wsg"], sm["nsg"]
            IMG = MP * 128
            r_qld = Res(2)
            r_qi32 = Res(2)
            r_qst = Res(2)
            r_qpt = Res(2)
            r_qev = Res(2, dma=True)
            r_pcs = Res(3, dma=True)
            r_fx = Res(2)
            r_fxn = Res(1)
            r_ftp = Res(1)
            r_fst = Res(1, dma=True)
            c0.wait(sy, 2)

            def w_load(ui, kind, r):
                sl = ui % 2
                buf = (v_qga, v_qgb)[sl]
                r_qld.acq(sy, ui)
                da = sm[("da0", "da1")[sl]]
                if kind in ("wg", "wu"):
                    wsrc = wg_my if kind == "wg" else wu_my
                    da.inc(sy.dma_start(buf[:, 0:M],
                                        wsrc[r * 128:(r + 1) * 128, :]), 16)
                else:
                    da.inc(sy.dma_start(
                        buf[:, 0:IMG],
                        AP(woutT, r * 128,
                           [[M, 128], [128 * M, MP], [1, 128]])), 16)

            pcq2 = [0]

            def wq_s1(ui, wsrc, r):
                sl = ui % 2
                buf = (v_qga, v_qgb)[sl]
                da = sm[("da0", "da1")[sl]]
                da.wait(ve)
                wsv.wait(ve)
                wq.inc(ve.tensor_reduce(mvt[:, sl:sl + 1], buf[:, 0:M], AX.X,
                                        ALU.max, apply_absolute_value=True), 1)
                wq.wait(ve)
                wq.inc(ve.reciprocal(mvt[:, 2 + sl:3 + sl], mvt[:, sl:sl + 1]), 1)

            def wq_s2(ui, svec, r, dst):
                sl = ui % 2
                buf = (v_qga, v_qgb)[sl]
                stg = (b_qsa, b_qsb)[sl]
                wq.wait(sc)
                wsv.inc(sc.mul(swl[:, svec + r:svec + r + 1],
                               mvt[:, sl:sl + 1], 1.0 / 127.0), 1)
                PH = min(512, M)
                r_qst.acq(gp, ui)
                wq.wait(ve)
                for j in range(-(-M // PH)):
                    cw = min(PH, M - j * PH)
                    pc = pcq2[0]
                    pcq2[0] += 1
                    io = (pc % 2) * 512
                    r_qi32.acq(ve, pc)
                    wq.inc(ve.tensor_scalar(sc_i32[:, io:io + cw],
                                            buf[:, j * PH:j * PH + cw],
                                            mvt[:, 2 + sl:3 + sl], 127.0,
                                            ALU.mult, ALU.mult), 1)
                    wq.wait(gp)
                    wqg.inc(gp.tensor_scalar(stg[:, j * PH:j * PH + cw],
                                             sc_i32[:, io:io + cw],
                                             127.0, -128.0, ALU.min, ALU.max), 1)
                    r_qi32.rel(pc, wqg)
                r_qld.rel(ui, wq)
                wqg.wait(te)
                r_qpt.acq(te, ui)
                for k in range(MP):
                    inst = te.transpose(
                        pt[:, sl * IMG + k * 128:sl * IMG + (k + 1) * 128],
                        stg[:, k * 128:(k + 1) * 128], id_bf[:, :])
                t1.inc(inst, 1)
                r_qst.rel(ui, t1)
                t1.wait(sc)
                ev = (b_qev0, b_qev1)[sl]
                r_qev.acq(sc, ui)
                inst = sc.activation(ev[:, 0:IMG], pt[:, sl * IMG:(sl + 1) * IMG],
                                     AF.Copy)
                ea.inc(inst, 1)
                r_qpt.rel(ui, ea)
                ea.wait(sy)
                wsg.inc(sy.dma_start(
                    AP(dst, r * 128 * IMG, [[IMG, 128], [1, IMG]]),
                    ev[:, 0:IMG]), 16)
                r_qev.rel(ui, wsg)

            def pc_unit(ui, psrc, n, dst):
                sl = ui % 2
                buf = (v_qga, v_qgb)[sl]
                stg = (b_pca, b_pcb, b_pcc)[ui % 3]
                da = sm[("da0", "da1")[sl]]
                da.wait(ve)
                r_pcs.acq(ve, ui)
                wq.inc(ve.tensor_copy(stg[:, 0:IMG], buf[:, 0:IMG]), 1)
                r_qld.rel(ui, wq)
                wq.wait(sy)
                ws.inc(sy.dma_start(
                    AP(dst, n * 128 * IMG, [[IMG, 128], [1, IMG]]),
                    stg[:, 0:IMG]), 16)
                r_pcs.rel(ui, ws)

            def p1_load(t):
                sl = t % 2
                buf = (v_fxa, v_fxb)[sl]
                r_fx.acq(sy, t)
                db = sm[("db0", "db1")[sl]]
                db.inc(sy.dma_start(buf[:, 0:M], xb[t * 128:(t + 1) * 128, :]), 16)

            def p1_tile(t):
                sl = t % 2
                buf = (v_fxa, v_fxb)[sl]
                db = sm[("db0", "db1")[sl]]
                db.wait(ve)
                for g in range(GN):
                    inst = ve.bn_stats(stats[:, 6 * g:6 * g + 6],
                                       buf[:, g * GSZ:min(M, (g + 1) * GSZ)])
                v1.inc(inst, 1)
                v1.wait(ve)
                s1.wait(ve)
                inst = ve.bn_aggr(mvt[:, 4:6], stats[:, 0:6 * GN])
                v1.inc(inst, 1)
                v1.wait(sc)
                inst = sc.activation(mvt[:, 6:7], mvt[:, 5:6], AF.Sqrt,
                                     bias=c_cst[:, 0:1])
                s1.inc(inst, 1)
                return buf

            def p1_tile_fin(t, buf):
                s1.wait(ve)
                inst = ve.reciprocal(mvt[:, 7:8], mvt[:, 6:7])
                v1.inc(inst, 1)
                v1.wait(ve)
                r_fxn.acq(ve, 0)
                inst = ve.tensor_scalar(v_fxn[:, 0:M], buf[:, 0:M], mvt[:, 4:5],
                                        mvt[:, 7:8], ALU.subtract, ALU.mult)
                v1.inc(inst, 1)
                r_fx.rel(t, v1)
                v1.wait(ve)
                inst = ve.tensor_reduce(amax[:, t:t + 1], v_fxn[:, 0:M], AX.X,
                                        ALU.max, apply_absolute_value=True)
                v1.inc(inst, 1)
                v1.wait(te)
                r_ftp.acq(te, 0)
                for k in range(MP):
                    bank = pb[0] if k < 4 else pb[1]
                    inst = te.transpose(bank[:, (k % 4) * 128:(k % 4 + 1) * 128],
                                        v_fxn[:, k * 128:(k + 1) * 128],
                                        id_f32[:, :])
                t1.inc(inst, 1)
                r_fxn.rel(0, t1)
                t1.wait(sc)
                tl = t % 2
                if tl == 0:
                    r_fst.acq(sc, 0)
                inst = sc.copy(
                    AP(af, v_fst.off + tl * 128,
                       [[AFC, 128], [256, min(4, MP)], [1, 128]]),
                    pb[0][:, 0:min(512, M)])
                if MP > 4:
                    inst = sc.copy(
                        AP(af, v_fst.off + 1024 + tl * 128,
                           [[AFC, 128], [256, 4], [1, 128]]),
                        pb[1][:, 0:M - 512])
                e1.inc(inst, 1)
                r_ftp.rel(0, e1)
                if tl == 1 or t == LT - 1:
                    e1.wait(sy)
                    t0 = t - tl
                    n1.inc(sy.dma_start(
                        AP(xnfm_h, t0 * 128,
                           [[L, 128], [128 * L, MP], [1, (tl + 1) * 128]]),
                        AP(af, v_fst.off,
                           [[AFC, 128], [256, MP], [1, (tl + 1) * 128]])), 16)
                    r_fst.rel(0, n1)

            wunits = ([("wg", r) for r in range(TPC)] +
                      [("wu", r) for r in range(TPC)] +
                      [("pco", n) for n in range(MP)])

            def emit_wunit(i, kind, r):
                if kind == "wg":
                    wq_s1(i, wg_my, r)
                    wq_s2(i, 0, r, wgT_src)
                elif kind == "wu":
                    wq_s1(i, wu_my, r)
                    wq_s2(i, TPC, r, wuT_src)
                if i == 2 * TPC - 1:
                    wsv.wait(sy)
                    nsg.inc(sy.dma_start(sws_src[:, :], swl[:, 0:2 * TPC]), 16)
                    wsg.wait(gp)
                    nsg.wait(gp)
                    cg.inc(gp.collective_compute(
                        "AllGather", ALU.bypass, replica_groups=rg,
                        ins=[sws_src.ap().opt()], outs=[sws_dst.ap().opt()]), 1)
                    cg.inc(gp.collective_compute(
                        "AllGather", ALU.bypass, replica_groups=rg,
                        ins=[wgT_src.ap().opt()], outs=[wgT_bh.ap().opt()]), 1)
                    cg.inc(gp.collective_compute(
                        "AllGather", ALU.bypass, replica_groups=rg,
                        ins=[wuT_src.ap().opt()], outs=[wuT_bh.ap().opt()]), 1)
                elif kind == "pco":
                    pc_unit(i, woutT, r, woutT_bh)
                if i + 2 < len(wunits):
                    w_load(i + 2, *wunits[i + 2])

            for k in range(MP):
                wqs.inc(gp.dma_start(
                    View(xint, 0, NP * MP * 128)[:, k * 2 * M:(k + 1) * 2 * M],
                    AP(winT, k * 128 * 2 * M, [[2 * M, 128], [1, 2 * M]])), 16)
            for _w in range(min(2, len(wunits))):
                w_load(_w, *wunits[_w])
            p1_load(0)
            if LT > 1:
                p1_load(1)
            wi = 0
            for t in range(LT):
                buf = p1_tile(t)
                for _ in range(2):
                    if wi < len(wunits):
                        emit_wunit(wi, *wunits[wi])
                        wi += 1
                p1_tile_fin(t, buf)
                if t + 2 < LT:
                    p1_load(t + 2)
                if t % 4 == 3 and wi < len(wunits):
                    emit_wunit(wi, *wunits[wi])
                    wi += 1
            while wi < len(wunits):
                emit_wunit(wi, *wunits[wi])
                wi += 1

            v1.wait(ve)
            inst = ve.tensor_reduce(amax[:, LT:LT + 1], amax[:, 0:LT], AX.X,
                                    ALU.max)
            v1.inc(inst, 1)
            v1.wait(sy)
            n1.inc(sy.dma_start(arx_src[:, :], amax[:, LT:LT + 1]), 16)
            cc = sm["cc"]
            n1.wait(gp)
            cc.inc(gp.collective_compute(
                "AllReduce", ALU.max, replica_groups=rg,
                ins=[arx_src.ap().opt()], outs=[arx_dst.ap().opt()]), 1)

            barrier()

            xa, va, sa = sm["xa"], sm["va"], sm["sa"]
            scn = sm["sc"]
            ta, wa, ma, pa = sm["ta"], sm["wa"], sm["ma"], sm["pa"]
            r_xt = Res(4)
            r_pre = Res(2)
            r_apt = Res(1)
            r_sl = Res(2)
            r_bk = Res(3)
            r_u = Res(1)
            NPAIR = -(-CT // 2) * NCH

            def pa_load(gp_):
                if gp_ >= NPAIR:
                    return
                tt0 = 2 * (gp_ % (-(-CT // 2)))
                t0 = (gp_ // (-(-CT // 2))) * CT + tt0
                pair = tt0 + 1 < CT
                xs0, xs1 = (v_xa, v_xb) if gp_ % 2 == 0 else (v_x4, v_x5)
                das = (sm["da0"], sm["da1"]) if gp_ % 2 == 0 else \
                      (sm["da2"], sm["da3"])
                r_xt.acq(sy, 2 * gp_)
                das[0].inc(sy.dma_start(xs0[:, 0:M],
                                        xt[t0 * 128:(t0 + 1) * 128, :]), 16)
                if pair:
                    r_xt.acq(sy, 2 * gp_ + 1)
                    das[1].inc(sy.dma_start(
                        xs1[:, 0:M], xt[(t0 + 1) * 128:(t0 + 2) * 128, :]), 16)

            PAIRS = -(-CT // 2)

            def ln_stage(src0, src1, pair, q8):
                va.wait(ve)
                sa.wait(ve)
                for g in range(GN):
                    inst = ve.bn_stats(stats[:, 6 * g:6 * g + 6],
                                       src0[:, g * GSZ:min(M, (g + 1) * GSZ)])
                if pair:
                    for g in range(GN):
                        inst = ve.bn_stats(
                            stats[:, 6 * GN + 6 * g:6 * GN + 6 * g + 6],
                            src1[:, g * GSZ:min(M, (g + 1) * GSZ)])
                va.inc(inst, 1)
                va.wait(ve)
                inst = ve.bn_aggr(mvt[:, q8:q8 + 2], stats[:, 0:6 * GN])
                if pair:
                    inst = ve.bn_aggr(mvt[:, q8 + 2:q8 + 4],
                                      stats[:, 6 * GN:12 * GN])
                va.inc(inst, 1)

            def st_sqrt(pair, q8):
                va.wait(sc)
                if pair:
                    inst = sc.activation(mvt[:, q8 + 4:q8 + 6],
                                         mvt[:, q8 + 1:q8 + 4:2],
                                         AF.Sqrt, bias=c_cst[:, 0:1])
                else:
                    inst = sc.activation(mvt[:, q8 + 4:q8 + 5],
                                         mvt[:, q8 + 1:q8 + 2],
                                         AF.Sqrt, bias=c_cst[:, 0:1])
                sa.inc(inst, 1)

            def ln_apply(src0, src1, dst0, dst1, pair, q8):
                sa.wait(ve)
                w = 2 if pair else 1
                inst = ve.reciprocal(mvt[:, q8 + 6:q8 + 6 + w],
                                     mvt[:, q8 + 4:q8 + 4 + w])
                va.inc(inst, 1)
                va.wait(ve)
                inst = ve.tensor_scalar(dst0[:, 0:M], src0[:, 0:M],
                                        mvt[:, q8:q8 + 1],
                                        mvt[:, q8 + 6:q8 + 7],
                                        ALU.subtract, ALU.mult)
                if pair:
                    inst = ve.tensor_scalar(dst1[:, 0:M], src1[:, 0:M],
                                            mvt[:, q8 + 2:q8 + 3],
                                            mvt[:, q8 + 7:q8 + 8],
                                            ALU.subtract, ALU.mult)
                va.inc(inst, 1)

            def ln_steps(ch):
                steps = []
                nrm_v = (b_slA, b_slB)[ch % 2]
                for pp in range(PAIRS):
                    gp_ = ch * PAIRS + pp
                    tt0 = 2 * pp
                    pair = tt0 + 1 < CT
                    q8 = (gp_ % 2) * 8
                    xs0, xs1 = (v_xa, v_xb) if gp_ % 2 == 0 else (v_x4, v_x5)
                    das = (sm["da0"], sm["da1"]) if gp_ % 2 == 0 else \
                          (sm["da2"], sm["da3"])

                    def s_ln1(gp_=gp_, pair=pair, q8=q8, xs0=xs0, xs1=xs1,
                              das=das):
                        das[0].wait(ve)
                        if pair:
                            das[1].wait(ve)
                        ln_stage(xs0, xs1, pair, q8)
                        pa_load(gp_ + 2)
                    steps.append(s_ln1)
                    steps.append(lambda pair=pair, q8=q8: st_sqrt(pair, q8))

                    def s_ap1(gp_=gp_, pair=pair, q8=q8, xs0=xs0, xs1=xs1):
                        ln_apply(xs0, xs1, v_xn, v_ln2, pair, q8)
                        r_xt.rel(2 * gp_, va)
                        if pair:
                            r_xt.rel(2 * gp_ + 1, va)
                        ln_stage(v_xn, v_ln2, pair, q8)
                    steps.append(s_ap1)
                    steps.append(lambda pair=pair, q8=q8: st_sqrt(pair, q8))

                    def s_ap2(pair=pair, q8=q8):
                        r_pre.acq(ve, 0)
                        if pair:
                            r_pre.acq(ve, 1)
                        ln_apply(v_xn, v_ln2, b_preA, b_preB, pair, q8)
                    steps.append(s_ap2)

                    def s_tr(pp=pp, tt0=tt0, pair=pair):
                        va.wait(te)
                        if pp == 0:
                            r_apt.acq(te, 0)
                        for k in range(MP):
                            inst = te.transpose(
                                pt[:, k * TC + tt0 * 128:
                                   k * TC + (tt0 + 1) * 128],
                                b_preA[:, k * 128:(k + 1) * 128], id_bf[:, :])
                        if pair:
                            for k in range(MP):
                                inst = te.transpose(
                                    pt[:, k * TC + (tt0 + 1) * 128:
                                       k * TC + (tt0 + 2) * 128],
                                    b_preB[:, k * 128:(k + 1) * 128],
                                    id_bf[:, :])
                        ta.inc(inst, 1)
                        r_pre.rel(0, ta)
                        if pair:
                            r_pre.rel(1, ta)
                    steps.append(s_tr)

                def s_gb(ch=ch, nrm_v=nrm_v):
                    ta.wait(ve)
                    r_sl.acq(ve, ch)
                    for k in range(MP):
                        inst = ve.tensor_scalar(nrm_v[:, k * TC:(k + 1) * TC],
                                                pt[:, k * TC:(k + 1) * TC],
                                                c_gam[:, k:k + 1],
                                                c_bet[:, k:k + 1],
                                                ALU.mult, ALU.add)
                    ea.inc(inst, 1)
                    r_apt.rel(0, ea)
                steps.append(s_gb)
                return steps

            pa_load(0)
            pa_load(1)
            pa_q = []
            pa_q.extend(ln_steps(0))
            while pa_q:
                pa_q.pop(0)()
            sm["wqs"].wait(te)
            for ch in range(NCH):
                nrm_v = (b_slA, b_slB)[ch % 2]
                xp_v = (b_slA, b_slB)[(ch + 1) % 2]
                if ch + 1 < NCH:
                    pa_q.extend(ln_steps(ch + 1))
                r_sl.acq(sc, ch + 1)
                for n in range(NP):
                    if pa_q:
                        pa_q.pop(0)()
                    if n < 4 and pa_q:
                        pa_q.pop(0)()
                    if n == MP:
                        pa.wait(te)
                        r_u.acq(te, 0)
                        for k in range(MP):
                            inst = te.matmul(pb[3][0:S, 0:TC],
                                             wxp_sb[:, k * S:(k + 1) * S],
                                             xp_v[:, k * TC:(k + 1) * TC],
                                             start=(k == 0), stop=(k == MP - 1))
                        ma.inc(inst, 1)
                        r_sl.rel(ch + 1, ma)
                        ma.wait(ve)
                        inst = ve.tensor_scalar(ubuf[0:S, ch * TC:(ch + 1) * TC],
                                                pb[3][0:S, 0:TC],
                                                c_bxp[0:S, 0:1], None, ALU.add)
                        va.inc(inst, 1)
                        r_u.rel(0, va)
                        va.wait(ve)
                        if ch > 0:
                            scn.wait(ve)
                        p1o = v_p1s[(ch + 1) % 2]
                        p1n = v_p1s[ch % 2]
                        for b in range(B):
                            init = 0.0 if ch == 0 else \
                                p1o[0:S, TC - B + b:TC - B + b + 1]
                            inst = ve.tensor_tensor_scan(
                                p1n[0:S, b:TC:B], c_arep[0:S, 0:TC // B],
                                ubuf[0:S, ch * TC + b:(ch + 1) * TC:B],
                                init, ALU.mult, ALU.add)
                        scn.inc(inst, 1)
                    wtile = View(xint, 0, NP * MP * 128)
                    ea.wait(te)
                    bank = pb[n % 3]
                    r_bk.acq(te, n)
                    for k in range(MP):
                        co = k * 2 * M + n * 128
                        inst = te.matmul(bank[:, 0:TC],
                                         wtile[:, co:co + 128],
                                         nrm_v[:, k * TC:(k + 1) * TC],
                                         start=(k == 0), stop=(k == MP - 1))
                    ma.inc(inst, 1)
                    if n == NP - 1:
                        r_sl.rel(ch, ma)
                    ma.wait(sc)
                    if n < MP:
                        inst = sc.activation(xp_v[:, n * TC:(n + 1) * TC],
                                             bank[:, 0:TC],
                                             AF.Silu, bias=c_bin[:, n:n + 1])
                    else:
                        inst = sc.activation(zs[:, (n - MP) * L + ch * TC:
                                                 (n - MP) * L + (ch + 1) * TC],
                                             bank[:, 0:TC],
                                             AF.Sigmoid, bias=c_bin[:, n:n + 1])
                    pa.inc(inst, 1)
                    r_bk.rel(n, pa)
                while pa_q:
                    pa_q.pop(0)()

            scn.wait(ve)
            inst = ve.tensor_copy(hloc[0:S, 0:B],
                                  v_p1s[(NCH - 1) % 2][0:S, TC - B:TC])
            scn.inc(inst, 1)
            scn.wait(sy)
            n1.inc(sy.dma_start(ag_src[:, :], hloc[0:S, 0:B]), 16)
            n1.wait(gp)
            cc.inc(gp.collective_compute(
                "AllGather", ALU.bypass, replica_groups=rg,
                ins=[ag_src.ap().opt()], outs=[ag_dst.ap().opt()]), 1)

            barrier()

            ld = sm["ld"]
            cc.wait(sy, 2)
            ld.inc(sy.dma_start(agb[0:S, 0:B * B],
                                AP(ag_dst, 0, [[B, S], [S * B, B], [1, B]])), 16)
            ld.wait(ve)
            inst = ve.tensor_scalar_mul(carry[0:S, 0:B], agb[0:S, 0:B],
                                        c_coef[0:S, 0:1])
            for k in range(1, B):
                scn.inc(inst, 1)
                scn.wait(ve)
                src = carry[0:S, 0:B] if k % 2 == 1 else carry[0:S, B:2 * B]
                dst = carry[0:S, B:2 * B] if k % 2 == 1 else carry[0:S, 0:B]
                inst = ve.scalar_tensor_tensor(dst, agb[0:S, k * B:(k + 1) * B],
                                               c_coef[0:S, k:k + 1], src,
                                               ALU.mult, ALU.add)
            cfin = carry[0:S, B:2 * B] if (B - 1) % 2 == 1 else carry[0:S, 0:B]
            scn.inc(inst, 1)

            q2, g2 = sm["q2"], sm["g2"]
            cc.wait(sy, 1)
            scn.wait(sy)
            ld.inc(sy.dma_start(sclx[:, 0:1], arx_dst[:, :]), 16)
            pm = sm["pm"]
            ld.wait(te)
            inst = te.transpose(pb[3][0:1, 0:128], sclx[:, 0:1], id_f32[:, :])
            pm.inc(inst, 1)
            pm.wait(ve)
            inst = ve.reduce_max(sclx[0:1, 0:1], pb[3][0:1, 0:128], AX.X)
            pm.inc(inst, 1)
            pm.wait(te)
            inst = te.matmul(pb[2][:, 0:1], c_ones[0:1, 0:128], sclx[0:1, 0:1],
                             start=True, stop=True)
            pm.inc(inst, 1)
            pm.wait(ve)
            inst = ve.tensor_copy(sclx[:, 1:2], pb[2][:, 0:1])
            pm.inc(inst, 1)
            pm.wait(ve)
            inst = ve.scalar_tensor_tensor(sclx[:, 2:3], c_sig[:, 0:1],
                                           1.0 / 127.0, sclx[:, 1:2],
                                           ALU.mult, ALU.mult)
            pm.inc(inst, 1)
            pm.wait(ve)
            inst = ve.reciprocal(sclx[:, 3:4], sclx[:, 2:3])
            pm.inc(inst, 1)
            pm.wait(ve)
            sm["cg"].wait(sy, 1)
            sm["nsg"].inc(sy.dma_start(
                sws[:, 0:16 * TPC],
                AP(sws_dst, 0, [[2 * TPC, 128], [128 * 2 * TPC, B],
                                [1, 2 * TPC]])), 16)
            sm["nsg"].wait(ve)
            inst = ve.tensor_scalar_mul(esgu[:, 0:16 * TPC], sws[:, 0:16 * TPC],
                                        sclx[:, 2:3])
            q2.inc(inst, 1)

            HF = min(1024, L)
            r_q2 = Res(4)
            r_qi2 = Res(2)
            pcq = [0]
            q2_q = []

            def q2_steps(k, hf):
                idx = k * (L // HF) + hf
                buf = (v_qga, v_qgb, v_qxc, v_qxd)[idx % 4]

                def s_load():
                    r_q2.acq(sy, idx)
                    da = sm[("da0", "da1", "da2", "da3")[idx % 4]]
                    da.inc(sy.dma_start(
                        buf[:, 0:HF],
                        AP(xnfm_h, k * 128 * L + hf * HF,
                           [[L, 128], [1, HF]])), 16)
                steps = [s_load]

                def mk_piece(j):
                    def f():
                        cw = min(QP, HF - j * QP)
                        off = k * L + hf * HF + j * QP
                        if j == 0:
                            da = sm[("da0", "da1", "da2", "da3")[idx % 4]]
                            da.wait(ve)
                        pc = pcq[0]
                        pcq[0] += 1
                        io = (pc % 2) * 512
                        r_qi2.acq(ve, pc)
                        inst = ve.tensor_scalar(sc_i32[:, io:io + cw],
                                                buf[:, j * QP:j * QP + cw],
                                                sclx[:, 3:4], None, ALU.mult)
                        q2.inc(inst, 1)
                        if j == -(-HF // QP) - 1:
                            r_q2.rel(idx, q2)
                        q2.wait(gp)
                        inst = gp.tensor_scalar(xint[:, off:off + cw],
                                                sc_i32[:, io:io + cw],
                                                127.0, -128.0,
                                                ALU.min, ALU.max)
                        wqg.inc(inst, 1)
                        r_qi2.rel(pc, wqg)
                    return f
                for j in range(-(-HF // QP)):
                    steps.append(mk_piece(j))
                return steps

            q2_lq = []
            for k in range(MP):
                for hf in range(L // HF):
                    st = q2_steps(k, hf)
                    q2_lq.append(st[0])
                    q2_q.extend(st[1:])
            NPC = -(-HF // QP)
            q2_st = [0, 0]

            def q2_tick():
                if q2_lq and (q2_st[0] - q2_st[1] // NPC) < 4:
                    q2_lq.pop(0)()
                    q2_st[0] += 1
                elif q2_q:
                    q2_q.pop(0)()
                    q2_st[1] += 1

            mc, ec, l2 = sm["mc"], sm["ec"], sm["l2"]
            r_bkc = Res(3)
            r_ysz = Res(1)
            r_wsc = Res(2)
            r_xfc = Res(2)
            WSUB = TC // B

            def pc_scan(ch):
                for b in range(B):
                    init = cfin[:, b:b + 1] if ch == 0 else \
                        b_hbf[0:S, ch * TC - B + b:ch * TC - B + b + 1]
                    if b == 0:
                        scn.wait(ve)
                    inst = ve.tensor_tensor_scan(
                        b_hbf[0:S, ch * TC + b:(ch + 1) * TC:B],
                        c_arep[0:S, 0:WSUB],
                        ubuf[0:S, ch * TC + b:(ch + 1) * TC:B],
                        init, ALU.mult, ALU.add)
                scn.inc(inst, 1)

            pc_scan(0)
            for ch in range(NCH):
                r_ysz.acq(ve, 0)
                for m in range(MP):
                    q2_tick()
                    bank = pb[m % 3]
                    r_bkc.acq(te, m)
                    if m == 0:
                        scn.wait(te)
                        if ch == 0:
                            pm.wait(te)
                    inst = te.matmul(bank[:, 0:TC],
                                     dm_sb[0:S, m * 128:(m + 1) * 128],
                                     b_hbf[0:S, ch * TC:(ch + 1) * TC],
                                     start=True, stop=True)
                    mc.inc(inst, 1)
                    mc.wait(ve)
                    inst = ve.tensor_tensor(
                        b_ysz[:, m * TC:(m + 1) * TC], bank[:, 0:TC],
                        zs[:, m * L + ch * TC:m * L + (ch + 1) * TC],
                        ALU.mult)
                    ec.inc(inst, 1)
                    r_bkc.rel(m, ec)
                if ch + 1 < NCH:
                    pc_scan(ch + 1)
                for mo in range(MP):
                    q2_tick()
                    widx = ch * MP + mo
                    wbuf = (b_cwa, b_cwb)[widx % 2]
                    r_wsc.acq(sy, widx)
                    if ch == 0 and mo == 0:
                        ws.wait(sy)
                    was = sm[("wa0", "wa1")[widx % 2]]
                    was.inc(sy.dma_start(
                        wbuf[:, 0:MP * 128],
                        AP(woutT_bh, mo * 128 * MP * 128,
                           [[MP * 128, 128], [1, MP * 128]])), 16)
                    xslot = (v_xfa, v_xfb)[widx % 2]
                    r_xfc.acq(sy, widx)
                    dbm = sm[("db0", "db1")[widx % 2]]
                    dbm.inc(sy.dma_start(
                        xslot[:, 0:TC],
                        AP(xnfm_h, mo * 128 * L + ch * TC,
                           [[L, 128], [1, TC]])), 16)
                    was.wait(te)
                    if mo == 0:
                        ec.wait(te)
                    bank = pb[mo % 3]
                    r_bkc.acq(te, mo)
                    for k in range(MP):
                        inst = te.matmul(bank[:, 0:TC],
                                         wbuf[:, k * 128:(k + 1) * 128],
                                         b_ysz[:, k * TC:(k + 1) * TC],
                                         start=(k == 0), stop=(k == MP - 1))
                    mc.inc(inst, 1)
                    r_wsc.rel(widx, mc)
                    if mo == MP - 1:
                        r_ysz.rel(0, mc)
                    mc.wait(ve)
                    dbm.wait(ve)
                    inst = ve.scalar_tensor_tensor(
                        zs[:, mo * L + ch * TC:mo * L + (ch + 1) * TC],
                        bank[:, 0:TC], c_bout[:, mo:mo + 1],
                        xslot[:, 0:TC], ALU.add, ALU.add)
                    ec.inc(inst, 1)
                    r_bkc.rel(mo, ec)
                    r_xfc.rel(widx, ec)
            while q2_q or q2_lq:
                q2_tick()

            barrier()

            m2, e2, h2 = sm["m2"], sm["e2"], sm["h2"]
            e2v = sm["e2v"]
            wq, wsv = sm["wq"], sm["wsv"]
            wqs, wqg = sm["wqs"], sm["wqg"]
            t1, ea = sm["t1"], sm["ea"]
            r_wg2 = Res(2)
            r_wu2 = Res(2)
            r_gs = Res(2)
            r_hb = Res(2, dma=True)
            r_bkg = Res(2)
            r_bku = Res(2)
            r_wdl = Res(1)
            r_wdq = Res(1)
            r_wdpt = Res(1)
            r_wdev = Res(1, dma=True)
            r_wi32 = Res(2)

            QW = 512
            NQ = -(-I // QW)

            def wd_steps(mh):
                steps = []

                def s_load():
                    r_wdl.acq(sy, 0)
                    wl.inc(sy.dma_start(v_wd3[:, 0:I],
                                        wd[mh * 128:(mh + 1) * 128, :]), 16)
                steps.append(s_load)

                def mk_red(j3):
                    def f():
                        if j3 == 0:
                            wl.wait(ve)
                        cw = min(1024, I - j3 * 1024)
                        wq.inc(ve.tensor_reduce(
                            mvt[:, j3:j3 + 1],
                            v_wd3[:, j3 * 1024:j3 * 1024 + cw],
                            AX.X, ALU.max, apply_absolute_value=True), 1)
                    return f
                for j3 in range(-(-I // 1024)):
                    steps.append(mk_red(j3))

                def s_scale():
                    wsv.wait(ve)
                    wq.wait(ve)
                    nred = -(-I // 1024)
                    wq.inc(ve.tensor_reduce(mvt[:, 3:4], mvt[:, 0:nred], AX.X,
                                            ALU.max), 1)
                    wq.wait(sc)
                    wsv.inc(sc.mul(swd[:, mh:mh + 1], mvt[:, 3:4],
                                   1.0 / 127.0), 1)
                    wsv.wait(ve)
                    wq.inc(ve.reciprocal(mvt[:, 4:5], swd[:, mh:mh + 1]), 1)
                steps.append(s_scale)

                def mk_piece(j):
                    def f():
                        cw = min(QW, I - j * QW)
                        io = (j % 2) * 512
                        if j == 0:
                            r_wdq.acq(gp, 0)
                            wq.wait(ve)
                        r_wi32.acq(ve, j % 2)
                        wq.inc(ve.tensor_scalar(sc_i32[:, io:io + cw],
                                                v_wd3[:, j * QW:j * QW + cw],
                                                mvt[:, 4:5], None, ALU.mult), 1)
                        if j == NQ - 1:
                            r_wdl.rel(0, wq)
                        wq.wait(gp)
                        wqg.inc(gp.tensor_scalar(b_wdq[:, j * QW:j * QW + cw],
                                                 sc_i32[:, io:io + cw],
                                                 127.0, -128.0,
                                                 ALU.min, ALU.max), 1)
                        r_wi32.rel(j % 2, wqg)
                    return f
                for j in range(NQ):
                    steps.append(mk_piece(j))

                def s_fin():
                    wqg.wait(te)
                    r_wdpt.acq(te, 0)
                    for k in range(IP):
                        inst = te.transpose(pt[:, k * 128:(k + 1) * 128],
                                            b_wdq[:, k * 128:(k + 1) * 128],
                                            id_bf[:, :])
                    t1.inc(inst, 1)
                    r_wdq.rel(0, t1)
                    t1.wait(sc)
                    r_wdev.acq(sc, 0)
                    inst = sc.activation(b_qev[:, 0:IP * 128], pt[:, 0:IP * 128],
                                         AF.Copy)
                    ea.inc(inst, 1)
                    r_wdpt.rel(0, ea)
                    ea.wait(sy)
                    ws.inc(sy.dma_start(
                        AP(wdT_bh, mh * 128 * IP * 128,
                           [[IP * 128, 128], [1, IP * 128]]),
                        b_qev[:, 0:IP * 128]), 16)
                    r_wdev.rel(0, ws)
                steps.append(s_fin)
                return steps

            wd_q = []
            wd_i = [0]

            def wd_tick():
                if not wd_q and wd_i[0] < MP:
                    wd_q.extend(wd_steps(wd_i[0]))
                    wd_i[0] += 1
                if wd_q:
                    wd_q.pop(0)()

            for r in range(IP):
                wbg = (b_g2a, b_g2b)[r % 2]
                wbu = (b_u2a, b_u2b)[r % 2]
                r_wg2.acq(sy, r)
                r_wu2.acq(sy, r)
                if r == 0:
                    sm["cg"].wait(sy, 3)
                was = sm[("wa0", "wa1")[r % 2]]
                was.inc(sy.dma_start(
                    wbg[:, 0:MP * 128],
                    AP(wgT_bh, r * 128 * MP * 128,
                       [[MP * 128, 128], [1, MP * 128]])), 16)
                was.inc(sy.dma_start(
                    wbu[:, 0:MP * 128],
                    AP(wuT_bh, r * 128 * MP * 128,
                       [[MP * 128, 128], [1, MP * 128]])), 16)
                was.wait(te)
                if r == 0:
                    q2.wait(te)
                    wqg.wait(te)
                for ch in range(NCH):
                    idx = r * NCH + ch
                    bg, bu = (pb[0], pb[2]) if idx % 2 == 0 else (pb[1], pb[3])
                    r_bkg.acq(te, idx)
                    for k in range(MP):
                        inst = te.matmul(
                            bg[:, 0:TC], wbg[:, k * 128:(k + 1) * 128],
                            xint[:, k * L + ch * TC:k * L + (ch + 1) * TC],
                            start=(k == 0), stop=(k == MP - 1))
                    m2.inc(inst, 1)
                    g_t = m2.n
                    r_bku.acq(te, idx)
                    for k in range(MP):
                        inst = te.matmul(
                            bu[:, 0:TC], wbu[:, k * 128:(k + 1) * 128],
                            xint[:, k * L + ch * TC:k * L + (ch + 1) * TC],
                            start=(k == 0), stop=(k == MP - 1))
                    m2.inc(inst, 1)
                    if ch == NCH - 1:
                        r_wg2.rel(r, m2)
                        r_wu2.rel(r, m2)
                    m2.wait(sc, g_t)
                    gbuf = (v_gsa, v_gsb)[idx % 2]
                    r_gs.acq(sc, idx)
                    gc = (r // TPC) * 2 * TPC + (r % TPC)
                    inst = sc.activation(gbuf[:, 0:TC], bg[:, 0:TC],
                                         AF.Silu, scale=esgu[:, gc:gc + 1])
                    e2.inc(inst, 1)
                    r_bkg.rel(idx, e2)
                    e2.wait(ve)
                    m2.wait(ve)
                    hbuf = (v_hsa, v_hsb)[idx % 2]
                    r_hb.acq(ve, idx)
                    inst = ve.scalar_tensor_tensor(
                        hbuf[:, 0:TC], bu[:, 0:TC],
                        esgu[:, gc + TPC:gc + TPC + 1], gbuf[:, 0:TC],
                        ALU.mult, ALU.mult)
                    e2v.inc(inst, 1)
                    r_bku.rel(idx, e2v)
                    r_gs.rel(idx, e2v)
                    e2v.wait(ve)
                    inst = ve.tensor_reduce(hmax[:, idx:idx + 1], hbuf[:, 0:TC],
                                            AX.X, ALU.max,
                                            apply_absolute_value=True)
                    g2.inc(inst, 1)
                    e2v.wait(sc)
                    h2.inc(sc.dma_start(h_h[r * 128:(r + 1) * 128,
                                            ch * TC:(ch + 1) * TC],
                                        hbuf[:, 0:TC]), 16)
                    r_hb.rel(idx, h2)
                    if idx % 3 != 2:
                        wd_tick()
            while wd_q or wd_i[0] < MP:
                wd_tick()
            g2.wait(ve)
            inst = ve.tensor_reduce(hmax[:, IP * NCH:IP * NCH + 1],
                                    hmax[:, 0:IP * NCH], AX.X, ALU.max)
            q2.inc(inst, 1)
            q2.wait(sy)
            n1.inc(sy.dma_start(arh_src[:, :], hmax[:, IP * NCH:IP * NCH + 1]), 16)
            n1.wait(gp)
            cc.inc(gp.collective_compute(
                "AllReduce", ALU.max, replica_groups=rg,
                ins=[arh_src.ap().opt()], outs=[arh_dst.ap().opt()]), 1)

            barrier()

            m3, e3, q3, o3 = sm["m3"], sm["e3"], sm["q3"], sm["o3"]
            es3, ev3 = sm["es3"], sm["ev3"]
            l2 = sm["l2"]
            cc.wait(gp, 3)
            l2.inc(gp.dma_start(sclh[:, 0:1], arh_dst[:, :]), 16)
            pm = sm["pm"]
            l2.wait(te)
            inst = te.transpose(pb[3][0:1, 0:128], sclh[:, 0:1], id_f32[:, :])
            pm.inc(inst, 1)
            pm.wait(ve)
            inst = ve.reduce_max(sclh[0:1, 0:1], pb[3][0:1, 0:128], AX.X)
            pm.inc(inst, 1)
            pm.wait(te)
            inst = te.matmul(pb[2][:, 0:1], c_ones[0:1, 0:128], sclh[0:1, 0:1],
                             start=True, stop=True)
            pm.inc(inst, 1)
            pm.wait(ve)
            inst = ve.tensor_copy(sclh[:, 1:2], pb[2][:, 0:1])
            pm.inc(inst, 1)
            pm.wait(ve)
            inst = ve.scalar_tensor_tensor(sclh[:, 2:3], c_sig[:, 1:2],
                                           1.0 / 127.0, sclh[:, 1:2],
                                           ALU.mult, ALU.mult)
            pm.inc(inst, 1)
            pm.wait(ve)
            inst = ve.reciprocal(sclh[:, 3:4], sclh[:, 2:3])
            pm.inc(inst, 1)
            pm.wait(ve)
            sm["wsv"].wait(ve)
            inst = ve.tensor_scalar_mul(esd[:, 0:MP], swd[:, 0:MP], sclh[:, 2:3])
            q3.inc(inst, 1)

            r_h3 = Res(4)
            r_hq = Res(3)
            r_i32 = Res(4)
            r_bank = Res(4)
            r_g3 = Res(1)
            r_t3 = Res(1)
            r_pt3 = Res(2)
            r_x3 = Res(2)
            r_o3 = Res(1, dma=True)
            qg = sm["g2"]
            h2.wait(sy)
            ws.wait(sy)
            wdl = sm["wa0"]
            wdl.inc(sy.dma_start(
                b_wdr[:, 0:MP * IP * 128],
                AP(wdT_bh, 0, [[IP * 128, 128], [128 * IP * 128, MP],
                               [1, IP * 128]])), 16)
            ldb = [0]
            q3_q = []
            qg_marks = {}

            def quant_steps(ch):
                hqv = b_hqs[ch % 3]
                g0s = list(range(0, IP, 2))

                def mk_load(gi):
                    g0 = g0s[gi]
                    sl = ldb[0] % 4
                    ldb[0] += 1

                    def f():
                        gn = min(2, IP - g0)
                        if g0 == 0:
                            r_hq.acq(sy, ch)
                        r_h3.acq(sy, sl)
                        da = sm[("da0", "da1", "da2", "da3")[sl]]
                        da.inc(sy.dma_start(
                            v_h3s[sl][:, 0:gn * TC3],
                            AP(h_h, g0 * 128 * L + ch * TC3,
                               [[L, 128], [128 * L, gn], [1, TC3]])), 16)
                    return f, sl

                def mk_piece(gi, sl):
                    g0 = g0s[gi]

                    def f():
                        gn = min(2, IP - g0)
                        buf = v_h3s[sl]
                        da = sm[("da0", "da1", "da2", "da3")[sl]]
                        da.wait(ve)
                        for i in range(gn):
                            r = g0 + i
                            r_i32.acq(ve, r % 4)
                            q3.inc(ve.tensor_scalar(
                                sc_i32[:, (r % 4) * TC3:(r % 4 + 1) * TC3],
                                buf[:, i * TC3:(i + 1) * TC3],
                                sclh[:, 3:4], None, ALU.mult), 1)
                        r_h3.rel(sl, q3)
                        q3.wait(gp)
                        qg.inc(gp.tensor_scalar(
                            hqv[:, g0 * TC3:(g0 + gn) * TC3],
                            sc_i32[:, (g0 % 4) * TC3:(g0 % 4 + gn) * TC3],
                            127.0, -128.0, ALU.min, ALU.max), 1)
                        for i2 in range(gn):
                            r_i32.rel(g0 + i2, qg)
                        if gi == len(g0s) - 1:
                            qg_marks[ch] = qg.n
                    return f

                loads = [mk_load(gi) for gi in range(len(g0s))]
                steps = []
                depth = min(2, len(g0s))
                for gi in range(depth):
                    steps.append(loads[gi][0])
                for gi in range(len(g0s)):
                    if gi + depth < len(g0s):
                        steps.append(loads[gi + depth][0])
                    steps.append(mk_piece(gi, loads[gi][1]))
                return steps

            def q3_tick():
                if q3_q:
                    q3_q.pop(0)()

            def mm_chunk(ch):
                hqv = b_hqs[ch % 3]
                poff = (ch % 2) * 2048
                for mh in range(MP):
                    if ch == 0 and mh == 0:
                        wdl.wait(te)
                    if mh == 0:
                        qg.wait(te, qg_marks[ch])
                    bank = pb[mh % 4]
                    r_bank.acq(te, mh % 4)
                    for r in range(IP):
                        inst = te.matmul(bank[:, 0:TC3],
                                         b_wdr[:, (mh * IP + r) * 128:
                                               (mh * IP + r + 1) * 128],
                                         hqv[:, r * TC3:(r + 1) * TC3],
                                         start=(r == 0), stop=(r == IP - 1))
                    m3.inc(inst, 1)
                    if mh == MP - 1:
                        r_hq.rel(ch, m3)
                    m3.wait(sc)
                    r_g3.acq(sc, 0)
                    inst = sc.activation(v_g3[:, 0:TC3], bank[:, 0:TC3], AF.Silu,
                                         scale=esd[:, mh:mh + 1])
                    es3.inc(inst, 1)
                    r_bank.rel(mh % 4, es3)
                    es3.wait(ve)
                    r_t3.acq(ve, 0)
                    ev3.inc(ve.tensor_tensor(
                        b_t3[:, mh * TC3:(mh + 1) * TC3],
                        zs[:, mh * L + ch * TC3:mh * L + (ch + 1) * TC3],
                        v_g3[:, 0:TC3], ALU.mult), 1)
                    r_g3.rel(0, ev3)
                    for _ in range(3):
                        q3_tick()
                ev3.wait(te)
                r_pt3.acq(te, ch)
                for mh in range(MP):
                    for tt in range(CT3):
                        inst = te.transpose(
                            pt[:, poff + (tt * MP + mh) * 128:
                               poff + (tt * MP + mh + 1) * 128],
                            b_t3[:, mh * TC3 + tt * 128:
                                 mh * TC3 + (tt + 1) * 128], id_bf[:, :])
                m3.inc(inst, 1)
                r_t3.rel(0, m3)
                return m3.n

            def fin_chunk(ch, m3_t):
                poff = (ch % 2) * 2048
                for tt in range(CT3):
                    t_glob = ch * CT3 + tt
                    xbuf = (v_x3a, v_x3b)[t_glob % 2]
                    r_x3.acq(sy, t_glob)
                    dbm = sm[("db0", "db1")[t_glob % 2]]
                    dbm.inc(sy.dma_start(xbuf[:, 0:M],
                                         xb[t_glob * 128:(t_glob + 1) * 128, :]), 16)
                    dbm.wait(ve)
                    m3.wait(ve, m3_t)
                    ov = (v_o3, v_o3b)[t_glob % 2]
                    osem = sm[("o3a", "o3b")[t_glob % 2]]
                    osem.wait(ve)
                    ev3.inc(ve.tensor_tensor(
                        ov[:, 0:M], xbuf[:, 0:M],
                        pt[:, poff + tt * MP * 128:poff + (tt + 1) * MP * 128],
                        ALU.add), 1)
                    r_x3.rel(t_glob, ev3)
                    if tt == CT3 - 1:
                        r_pt3.rel(ch, ev3)
                    ev3.wait(sy)
                    osem.inc(
                        sy.dma_start(out[t_glob * 128:(t_glob + 1) * 128, :],
                                     ov[:, 0:M]), 16)

            ve.memset(v_o3[:, 0:M], 0.0)
            ve.memset(v_o3b[:, 0:M], 0.0)
            q3.inc(ve.drain(), 1)
            q3_q.extend(quant_steps(0))
            while q3_q:
                q3_tick()
            if NC3 > 1:
                q3_q.extend(quant_steps(1))
            fin_pend = []
            for ch in range(NC3):
                if ch + 2 < NC3:
                    q3_q.extend(quant_steps(ch + 2))
                m3t = mm_chunk(ch)
                if fin_pend:
                    fin_chunk(*fin_pend.pop(0))
                fin_pend.append((ch, m3t))
                while ch + 1 >= NC3 and q3_q:
                    q3_tick()
            while fin_pend:
                fin_chunk(*fin_pend.pop(0))

            sm["o3a"].wait(real)
            sm["o3b"].wait(real)
            barrier()

        @block.sync
        def _(e):
            emit("sync", e)

        @block.vector
        def _(e):
            emit("vector", e)

        @block.scalar
        def _(e):
            emit("scalar", e)

        @block.gpsimd
        def _(e):
            emit("gpsimd", e)

        @block.tensor
        def _(e):
            emit("tensor", e)

    return nc


def _in_maps(cfg, x, W_in, W_xp, W_out, Dmat, wg_, wu_, wd_, b_in_, b_xp_,
             b_out_, gamma_, beta_, A_log_, clip_x_, clip_d_):
    L, M, S = cfg["L"], cfg["M"], cfg["S"]
    B = N_CORES
    W = L // B
    ITOT = wg_.shape[0]
    TPC = -(-(ITOT // 128) // B)

    def _shard(w, c):
        lo, hi = c * TPC * 128, min((c + 1) * TPC * 128, ITOT)
        pad = np.full((TPC * 128, w.shape[1]), 1.0, np.float32)
        pad[0:hi - lo] = w[lo:hi]
        return np.ascontiguousarray(pad)

    shared = dict(
        winT=np.ascontiguousarray(W_in.T), wxpT=np.ascontiguousarray(W_xp.T),
        woutT=np.ascontiguousarray(W_out.T), dmat=np.ascontiguousarray(Dmat),
        wd=np.ascontiguousarray(wd_),
        b_in=np.ascontiguousarray(b_in_), b_xp=np.ascontiguousarray(b_xp_),
        b_out=np.ascontiguousarray(b_out_),
        gamma=np.ascontiguousarray(gamma_), beta=np.ascontiguousarray(beta_),
        alog=np.ascontiguousarray(np.asarray(A_log_).reshape(S)),
        clipx=np.ascontiguousarray(np.asarray(clip_x_).reshape(1)),
        clipd=np.ascontiguousarray(np.asarray(clip_d_).reshape(1)),
    )
    maps = []
    for c in range(B):
        ev = np.full(B, 1e9, np.float32)
        for k in range(c):
            ev[k] = float(W * (c - 1 - k))
        m = dict(shared)
        m.update(
            wg_my=_shard(wg_, c), wu_my=_shard(wu_, c),
            xb=np.ascontiguousarray(x[c]),
            xt=np.ascontiguousarray(
                x[:, c * W:(c + 1) * W, :].transpose(1, 0, 2).reshape(L, M)),
            evec=ev,
        )
        maps.append(m)
    return maps


def kernel(L0_attn_x, L0_attn_A_log, L0_attn_Dmat, L0_attn_W_in, L0_attn_b_in,
           L0_attn_W_xp, L0_attn_b_xp, L0_attn_W_out, L0_attn_b_out,
           L0_attn_gamma, L0_attn_beta, L0_ffn_clip_x, L0_ffn_clip_down,
           L0_ffn_w_gate, L0_ffn_w_up, L0_ffn_w_down):
    cfg = CFG
    f32 = lambda a: np.asarray(a, dtype=np.float32)
    maps = _in_maps(cfg, f32(L0_attn_x), f32(L0_attn_W_in), f32(L0_attn_W_xp),
                    f32(L0_attn_W_out), f32(L0_attn_Dmat), f32(L0_ffn_w_gate),
                    f32(L0_ffn_w_up), f32(L0_ffn_w_down), f32(L0_attn_b_in),
                    f32(L0_attn_b_xp), f32(L0_attn_b_out), f32(L0_attn_gamma),
                    f32(L0_attn_beta), f32(L0_attn_A_log), f32(L0_ffn_clip_x),
                    f32(L0_ffn_clip_down))
    nc = build_nc(cfg)
    res = run_bass_kernel_spmd(nc, maps, core_ids=list(range(N_CORES)))
    return np.stack([res.results[c]["out"] for c in range(N_CORES)], axis=0)
